# version 17
# baseline (speedup 1.0000x reference)
import sys
sys.path.insert(0, "/opt/trn_rl_repo")
from contextlib import ExitStack
import numpy as np
import ml_dtypes
import concourse.bass as bass
import concourse.mybir as mybir
from concourse.bass_utils import run_bass_kernel_spmd
from concourse.bass_core_rust import IndirectOffsetOnAxis

F32 = mybir.dt.float32
BF16 = mybir.dt.bfloat16
I32 = mybir.dt.int32
AF = mybir.ActivationFunctionType
ALU = mybir.AluOpType
AX = mybir.AxisListType
BF = ml_dtypes.bfloat16

B, S, E = 4, 2048, 2048
RSQ = 0.08838834764831845
LN2 = 0.6931471805599453
INVLN2 = 1.4426950408889634
SLOT_LENS = (16, 14, 12, 10, 8, 6, 4, 2)
NKT = sum(SLOT_LENS)
ALLC = [[0, 1, 2, 3, 4, 5, 6, 7]]


def own_tiles(c):
    out = []
    for j in range(8):
        Lj = SLOT_LENS[j]
        heavy = (j % 2 == 0) == (c < 4)
        out.append((c % 4, (Lj - 1) if heavy else (Lj - 2)))
    return out


def slot_offsets():
    out, off = [], 0
    for L in SLOT_LENS:
        out.append((off, L))
        off += L
    return out


class SemC:
    def __init__(self, h):
        self.h = h
        self.n = 0


class Em:
    ENGS = ("sync", "gpsimd", "vector", "scalar", "tensor")
    SELF_SYNC = ("vector", "scalar")

    def __init__(self, nc, es):
        self.nc = nc
        self.es = es
        self.st = {k: [] for k in self.ENGS}
        self._sems = {}
        self._last = {}

    def sem(self, name):
        if isinstance(name, SemC):
            return name
        if name not in self._sems:
            h = self.es.enter_context(self.nc.semaphore(name))
            self._sems[name] = SemC(h)
        return self._sems[name]

    def op(self, eng, build, inc=None, k=1):
        if eng in self.SELF_SYNC:
            prev = self._last.get(eng)
            if prev is not None:
                self.st[eng].append(lambda e, h=prev[0], vv=prev[1]: e.wait_ge(h, vv))
            if inc is None:
                inc = eng + "_selfsync"
        if inc is not None:
            s = self.sem(inc)
            s.n += k
            self.st[eng].append(lambda e, b=build, h=s.h, kk=k: b(e).then_inc(h, kk))
            if eng in self.SELF_SYNC:
                self._last[eng] = (s.h, s.n)
        else:
            self.st[eng].append(lambda e, b=build: b(e))

    def dma(self, eng, out, in_, inc, **kw):
        self.op(eng, lambda e, o=out, i=in_, kw2=dict(kw): e.dma_start(out=o, in_=i, **kw2),
                inc, 16)

    def wait(self, eng, sem, val=None):
        s = self.sem(sem)
        v = s.n if val is None else val
        if v > 0:
            self.st[eng].append(lambda e, h=s.h, vv=v: e.wait_ge(h, vv))

    def barrier(self, sems):
        for eng in self.ENGS:
            for s in sems:
                self.wait(eng, s)

    def finalize(self):
        with self.nc.Block() as block:
            @block.sync
            def _(e):
                for f in self.st["sync"]:
                    f(e)

            @block.gpsimd
            def _(e):
                for f in self.st["gpsimd"]:
                    f(e)

            @block.vector
            def _(e):
                for f in self.st["vector"]:
                    f(e)

            @block.scalar
            def _(e):
                for f in self.st["scalar"]:
                    f(e)

            @block.tensor
            def _(e):
                for f in self.st["tensor"]:
                    f(e)



def emit_round(em, eng, v_ap, r_out, i_scr, f1, f2, inc=None):
    em.op(eng, lambda e: e.tensor_scalar(i_scr, v_ap, 0.5, None, ALU.add))
    em.op(eng, lambda e: e.tensor_tensor(f1, v_ap, i_scr, ALU.subtract))
    em.op(eng, lambda e: e.tensor_single_scalar(f2, f1, 0.5, ALU.is_ge))
    em.op(eng, lambda e: e.tensor_single_scalar(f1, f1, -0.5, ALU.is_lt))
    em.op(eng, lambda e: e.tensor_tensor(f2, f2, f1, ALU.subtract))
    em.op(eng, lambda e: e.tensor_tensor(r_out, f2, i_scr, ALU.add), inc=inc)


def emit_e4m3(em, bamax_ap, inv6ts_col, invts_col, bsq, invd, scr, sv, ss, bm3, bp3):
    V, SC = "vector", "scalar"
    f1, f2, f3, i1 = scr["f1"][:, :], scr["f2"][:, :], scr["f3"][:, :], scr["i1"][:, :]
    bq = bsq[:, :]
    iv = invd[:, :]
    em.op(V, lambda e: e.tensor_scalar(f1, bamax_ap, inv6ts_col, None, ALU.mult))
    em.op(V, lambda e: e.tensor_scalar(f1, f1, 448.0, 2.44140625e-4, ALU.min, ALU.max), inc=sv)
    em.wait(SC, sv)
    em.op(SC, lambda e: e.activation(f2, f1, AF.Ln), inc=ss)
    em.wait(V, ss)
    em.op(V, lambda e: e.tensor_single_scalar(f2, f2, INVLN2, ALU.mult))
    em.op(V, lambda e: e.tensor_scalar(i1, f2, 0.0, None, ALU.add))
    em.op(V, lambda e: e.tensor_tensor(f3, f2, i1, ALU.subtract))
    em.op(V, lambda e: e.tensor_single_scalar(f3, f3, 0.0, ALU.is_lt))
    em.op(V, lambda e: e.tensor_tensor(f2, i1, f3, ALU.subtract))
    em.op(V, lambda e: e.tensor_scalar(f2, f2, -6.0, 8.0, ALU.max, ALU.min), inc=sv)
    em.wait(SC, sv)
    em.op(SC, lambda e: e.activation(f3, f2, AF.Exp, bias=bm3, scale=LN2))
    em.op(SC, lambda e: e.activation(f2, f2, AF.Exp, bias=bp3, scale=-LN2), inc=ss)
    em.wait(V, ss)
    em.op(V, lambda e: e.tensor_tensor(f1, f1, f2, ALU.mult))
    emit_round(em, V, f1, f1, i1, f2, bq)
    em.op(V, lambda e: e.tensor_tensor(bq, f1, f3, ALU.mult))
    em.op(V, lambda e: e.tensor_single_scalar(f2, bq, 0.0, ALU.is_equal))
    em.op(V, lambda e: e.tensor_tensor(f2, bq, f2, ALU.add))
    em.op(V, lambda e: e.reciprocal(f2, f2))
    em.op(V, lambda e: e.tensor_scalar(iv, f2, invts_col, None, ALU.mult))


def emit_e2m1(em, xin_ap, invd, bsq, xdq_ap, scr, sv, ss, inc=None):
    V, SC = "vector", "scalar"
    af, tf = scr["af"][:, :], scr["tf"][:, :]
    a2, c2, c4, f2 = (scr[k][:, :] for k in ("a2", "c2", "c4", "f2"))
    i1 = scr["i1"][:, :]

    def b3(t):
        return t[:, 0:128].rearrange("p (b one) -> p b one", one=1).broadcast_to([128, 128, 16])

    def v3(t):
        return t.rearrange("p (b k) -> p b k", k=16)

    em.op(V, lambda e: e.tensor_tensor(v3(af), v3(xin_ap), b3(invd), ALU.mult))
    em.op(V, lambda e: e.tensor_scalar(af, af, 6.0, -6.0, ALU.min, ALU.max), inc=sv)
    em.wait(SC, sv)
    em.op(SC, lambda e: e.activation(a2, af, AF.Square), inc=ss)
    em.wait(V, ss)
    em.op(V, lambda e: e.tensor_single_scalar(c2, a2, 4.0, ALU.is_lt))
    em.op(V, lambda e: e.tensor_single_scalar(c4, a2, 16.0, ALU.is_lt))
    em.op(V, lambda e: e.tensor_scalar(a2, c4, 0.5, 0.5, ALU.mult, ALU.add))
    em.op(V, lambda e: e.tensor_tensor(a2, a2, c2, ALU.add))
    em.op(V, lambda e: e.tensor_tensor(tf, af, a2, ALU.mult))
    em.op(V, lambda e: e.tensor_scalar(i1, tf, 0.5, None, ALU.add))
    em.op(V, lambda e: e.tensor_tensor(af, tf, i1, ALU.subtract))
    em.op(V, lambda e: e.tensor_single_scalar(f2, af, 0.5, ALU.is_ge))
    em.op(V, lambda e: e.tensor_single_scalar(af, af, -0.5, ALU.is_lt))
    em.op(V, lambda e: e.tensor_tensor(f2, f2, af, ALU.subtract))
    em.op(V, lambda e: e.tensor_tensor(tf, f2, i1, ALU.add))
    em.op(V, lambda e: e.tensor_scalar(c4, c4, -1.0, 2.0, ALU.mult, ALU.add))
    em.op(V, lambda e: e.scalar_tensor_tensor(c4, c2, -0.5, c4, ALU.mult, ALU.add))
    em.op(V, lambda e: e.tensor_tensor(tf, tf, c4, ALU.mult))
    em.op(V, lambda e: e.tensor_tensor(v3(xdq_ap), v3(tf), b3(bsq), ALU.mult), inc=inc)


def emit_int8_cols(em, amax_ap, inv_ap, scale_ap, s0, s1):
    V = "vector"
    em.op(V, lambda e: e.tensor_single_scalar(s0, amax_ap, 0.0, ALU.is_equal))
    em.op(V, lambda e: e.tensor_tensor(s1, amax_ap, s0, ALU.add))
    em.op(V, lambda e: e.reciprocal(s1, s1))
    em.op(V, lambda e: e.tensor_single_scalar(s1, s1, 127.0, ALU.mult))
    em.op(V, lambda e: e.scalar_tensor_tensor(inv_ap, s0, -126.0, s1, ALU.mult, ALU.add))
    em.op(V, lambda e: e.tensor_single_scalar(s1, amax_ap, 1.0 / 127.0, ALU.mult))
    em.op(V, lambda e: e.tensor_tensor(scale_ap, s1, s0, ALU.add))



def build_fused():
    nc = bass.Bass("TRN2", target_bir_lowering=False)
    D = {}

    def din(n, sh, dt=F32):
        D[n] = nc.dram_tensor(n, list(sh), dt, kind="ExternalInput").ap()

    def dout(n, sh, dt=F32):
        D[n] = nc.dram_tensor(n, list(sh), dt, kind="ExternalOutput").ap()

    def dint(n, sh, dt=F32):
        h = nc.dram_tensor(n, list(sh), dt)
        D[n] = h.ap()
        return h

    din("x_own", (1024, E))
    din("wq", (E, 16, 128))
    din("wk", (E, 4, 128))
    din("wv", (E, 4, 128))
    din("bq_row", (1, E))
    din("bk_row", (1, 512))
    din("bv_row", (1, 512))
    din("ffn_bias0", (1, E))
    din("xts", (1, 1))
    din("ffn_bias1", (1, E))
    din("w0_full", (E, E))
    din("w0_own", (256, E))
    din("k1_own", (E, 256))
    din("wq1_own", (256, E))
    din("wk1_own", (256, E))
    din("wv1_own", (256, E))
    din("wo1_own", (256, E))
    din("ident", (128, 128), BF16)
    din("ident_f", (128, 128))
    din("maskT", (128, 128), BF16)
    din("kidx0", (128, NKT), I32)
    din("vidx0", (128, NKT), I32)
    din("kidx1", (128, NKT), I32)
    din("vidx1", (128, NKT), I32)
    dout("xo", (1024, E))

    h_cprep = dint("cprep", (6 * E, 256), BF16)
    h_gprep = dint("gprep", (8 * 6 * E, 256), BF16)
    h_cks = dint("cks", (256, 1))
    h_gks = dint("gks", (2048, 1))
    dint("wtr", (1, 4))
    dint("qfm_d0", (E, 1024), BF16)
    dint("qfm_d1", (E, 1024), BF16)
    dint("g_d0", (1024, E))
    dint("g_d1", (1024, E))
    dint("x1_d", (1024, E))
    h_ckv = [dint("ckv0", (2176, 520), BF16), dint("ckv1", (2176, 2080), BF16)]
    h_gkv = [dint("gkv0", (8 * 2176, 520), BF16), dint("gkv1", (8 * 2176, 2080), BF16)]

    with ExitStack() as es:
        em = Em(nc, es)
        sb = lambda n, sh, dt=F32: es.enter_context(nc.sbuf_tensor(n, sh, dt))

        id_sb = sb("id_sb", [128, 128], BF16)
        id_f = sb("id_f", [128, 128])
        msk = sb("msk", [128, 128], BF16)
        ones_f = sb("ones_f", [1, 512])
        ones_bf = sb("ones_bf", [1, 512], BF16)
        cst = sb("cst", [128, 4])

        cc = em.sem("cc")
        ld_id = em.sem("ld_id")
        st = em.sem("st")
        st_w = em.sem("st_w")
        pe = em.sem("pe")
        v = em.sem("v")
        scs = em.sem("scs")
        sc = em.sem("sc")
        sv = em.sem("sv")
        ss = em.sem("ss")
        evq = em.sem("evq")
        evt = em.sem("evt")
        gld = em.sem("gld")
        st_s = [em.sem("st_s0"), em.sem("st_s1")]
        st_g = em.sem("st_g")
        st_o = [em.sem("st_o0"), em.sem("st_o1")]
        ldk = [em.sem("ldk0"), em.sem("ldk1")]

        em.dma("sync", id_sb[:, :], D["ident"][:, :], ld_id)
        em.dma("sync", id_f[:, :], D["ident_f"][:, :], ld_id)
        em.dma("sync", msk[:, :], D["maskT"][:, :], ld_id)
        em.op("vector", lambda e: e.memset(ones_f[:, :], 1.0))
        em.op("vector", lambda e: e.memset(ones_bf[:, :], 1.0))
        em.op("vector", lambda e: e.memset(cst[:, 0:1], -3.0 * LN2))
        em.op("vector", lambda e: e.memset(cst[:, 1:2], 3.0 * LN2))
        em.op("vector", lambda e: e.memset(cst[:, 2:3], 1e-5))
        em.op("vector", lambda e: e.memset(cst[:, 3:4], 0.1), inc=v)
        em.wait("scalar", v)
        em.wait("tensor", ld_id)
        em.wait("vector", ld_id)

        marks = {}

        def stage_prep():
            esP = ExitStack()
            sbP = lambda n, sh, dt=F32: esP.enter_context(nc.sbuf_tensor(n, sh, dt))
            psP = lambda n, sh, dt=F32: esP.enter_context(nc.psum_tensor(n, sh, dt))
            wf = [sbP(f"p_wf{i}", [128, E]) for i in range(2)]
            wbf = sbP("p_wbf", [128, E], BF16)
            bamax = sbP("p_bamax", [128, E])
            colmax = sbP("p_colmax", [128, 1])
            rowmax = sbP("p_rowmax", [1, 128])
            wts_sb = sbP("p_wts", [1, 4])
            inv6ts_col = sbP("p_i6", [128, 1])
            invts_col = sbP("p_it", [128, 1])
            e4scr = {"f1": sbP("p_e41", [128, 128]), "f2": sbP("p_e42", [128, 128]),
                     "f3": sbP("p_e43", [128, 128]), "i1": sbP("p_e4i", [128, 128], I32)}
            pf_a = sbP("p_pfa", [128, E])
            pf_t = sbP("p_pft", [128, E])
            qscr = {k: sbP("p_q" + k, [128, E], BF16) for k in ("a2", "c2", "c4", "f2")}
            qscr["af"] = pf_a
            qscr["tf"] = pf_t
            qscr["i1"] = sbP("p_qi1", [128, E], I32)
            bsq = sbP("p_bsq", [128, 128])
            invd = sbP("p_invd", [128, 128])
            wdq = sbP("p_wdq", [128, E], BF16)
            stage = sbP("p_stage", [128, 16 * 256], BF16)
            kT = [sbP(f"p_kT{i}", [128, E]) for i in range(2)]
            kqb = sbP("p_kqb", [128, E], BF16)
            kamax = sbP("p_kamax", [128, 2])
            kinv = sbP("p_kinv", [128, 2])
            kscale = sbP("p_kscale", [128, 2])
            small = sbP("p_small", [128, 8])
            ps_t = [psP(f"p_pst{i}", [128, 1024], BF16) for i in range(2)]
            ps_tf = [psP(f"p_pstf{i}", [128, 512]) for i in range(2)]
            ps_c = psP("p_psc", [128, 512])
            ld_w0 = em.sem("ld_w0")
            ld_w1 = em.sem("ld_w1")

            cprep = D["cprep"]

            def store_stage_rows(iw):
                em.wait("sync", sc)
                for kj in range(16):
                    em.dma("sync", cprep[2048 * iw + 128 * kj:2048 * iw + 128 * (kj + 1), :],
                           stage[:, 256 * kj:256 * (kj + 1)], st)

            vmarks = []
            for ch in range(16):
                par = ch % 2
                if ch >= 2:
                    em.wait("sync", v, vmarks[ch - 2])
                em.dma("sync", wf[par][:, :], D["w0_full"][128 * ch:128 * (ch + 1), :],
                       ld_w0 if par == 0 else ld_w1)
                em.wait("vector", ld_w0 if par == 0 else ld_w1)
                em.op("vector", lambda e, c=ch, p=par: e.tensor_reduce(
                    bamax[:, 128 * c:128 * (c + 1)].rearrange("p (b k) -> p b k", k=16),
                    wf[p][:, :].rearrange("p (b k) -> p b k", k=16),
                    AX.X, ALU.max, apply_absolute_value=True), inc=v)
                vmarks.append(v.n)
            em.op("vector", lambda e: e.tensor_reduce(colmax[:, 0:1], bamax[:, :], AX.X,
                                                      ALU.max, apply_absolute_value=True), inc=v)
            em.wait("tensor", v)
            em.op("tensor", lambda e: e.transpose(ps_tf[0][0:1, 0:128], colmax[:, 0:1], id_f[:, :]), inc=pe)
            em.wait("vector", pe)
            em.op("vector", lambda e: e.tensor_copy(rowmax[0:1, :], ps_tf[0][0:1, 0:128]))
            em.op("vector", lambda e: e.tensor_reduce(wts_sb[0:1, 0:1], rowmax[0:1, :], AX.X,
                                                      ALU.max, apply_absolute_value=True))
            em.op("vector", lambda e: e.tensor_single_scalar(wts_sb[0:1, 0:1], wts_sb[0:1, 0:1],
                                                             1.0 / 2688.0, ALU.mult))
            em.op("vector", lambda e: e.tensor_single_scalar(wts_sb[0:1, 1:2], wts_sb[0:1, 0:1],
                                                             6.0, ALU.mult))
            em.op("vector", lambda e: e.reciprocal(wts_sb[0:1, 1:2], wts_sb[0:1, 1:2]))
            em.op("vector", lambda e: e.reciprocal(wts_sb[0:1, 2:3], wts_sb[0:1, 0:1]), inc=v)
            em.wait("sync", v)
            em.dma("sync", D["wtr"][0:1, 0:1], wts_sb[0:1, 0:1], st_w)
            marks["wts_store"] = st_w.n
            em.wait("tensor", v)
            em.op("tensor", lambda e: e.matmul(ps_c[:, 0:1], ones_f[0:1, 0:128], wts_sb[0:1, 1:2],
                                               start=True, stop=True))
            em.op("tensor", lambda e: e.matmul(ps_c[:, 1:2], ones_f[0:1, 0:128], wts_sb[0:1, 2:3],
                                               start=True, stop=True), inc=pe)
            em.wait("vector", pe)
            em.op("vector", lambda e: e.tensor_copy(inv6ts_col[:, 0:1], ps_c[:, 0:1]))
            em.op("vector", lambda e: e.tensor_copy(invts_col[:, 0:1], ps_c[:, 1:2]), inc=v)

            for cci in range(2):
                em.wait("sync", v, v.n)
                em.dma("sync", wf[0][:, :], D["w0_own"][128 * cci:128 * (cci + 1), :], ld_w0)
                em.wait("vector", ld_w0)
                em.op("vector", lambda e: e.tensor_reduce(
                    bamax[:, 0:128].rearrange("p (b k) -> p b k", k=16),
                    wf[0][:, :].rearrange("p (b k) -> p b k", k=16),
                    AX.X, ALU.max, apply_absolute_value=True))
                emit_e4m3(em, bamax[:, 0:128], inv6ts_col[:, 0:1], invts_col[:, 0:1],
                          bsq, invd, e4scr, sv, ss, cst[:, 0:1], cst[:, 1:2])
                emit_e2m1(em, wf[0][:, :], invd, bsq, wdq[:, :], qscr, sv, ss, inc=v)
                em.wait("tensor", v)
                for q4 in range(4):
                    pst = ps_t[q4 % 2]
                    em.wait("tensor", sc, sc.n - 1)
                    for j4 in range(4):
                        j = 4 * q4 + j4
                        em.op("tensor", lambda e, jj=j, p=pst, o=j4: e.transpose(
                            p[:, 128 * o:128 * (o + 1)], wdq[:, 128 * jj:128 * (jj + 1)],
                            id_sb[:, :]), inc=(pe if j4 == 3 else None))
                    em.wait("scalar", pe)
                    em.op("scalar", lambda e, q=q4, c=cci, p=pst: e.copy(
                        stage[:, :].rearrange("p (kj n) -> p kj n", n=256)[:, 4 * q:4 * (q + 1), 128 * c:128 * (c + 1)],
                        p[:, 0:512].rearrange("p (j o) -> p j o", o=128)), inc=sc)
            store_stage_rows(0)

            em.wait("sync", v, v.n)
            em.wait("sync", pe, pe.n)
            for dc in range(16):
                em.wait("sync", pe, pe.n)
                em.dma("sync", wf[1][:, 0:256], D["k1_own"][128 * dc:128 * (dc + 1), :], ld_w1)
                em.wait("tensor", ld_w1)
                em.wait("tensor", sc, sc.n)
                for fb in range(2):
                    em.op("tensor", lambda e, f=fb: e.transpose(
                        ps_tf[f][:, 0:128], wf[1][:, 128 * f:128 * (f + 1)], id_f[:, :]),
                        inc=(pe if fb == 1 else None))
                em.wait("scalar", pe)
                for fb in range(2):
                    em.op("scalar", lambda e, f=fb, d=dc: e.copy(
                        kT[f][:, 128 * d:128 * (d + 1)], ps_tf[f][:, 0:128]), inc=sc)
            em.wait("vector", sc)
            for fb in range(2):
                em.op("vector", lambda e, f=fb: e.tensor_reduce(
                    kamax[:, f:f + 1], kT[f][:, :], AX.X, ALU.max, apply_absolute_value=True))
                emit_int8_cols(em, kamax[:, fb:fb + 1], kinv[:, fb:fb + 1], kscale[:, fb:fb + 1],
                               small[:, 0:1], small[:, 1:2])
                em.op("vector", lambda e, f=fb: e.tensor_scalar(
                    pf_a[:, :], kT[f][:, :], kinv[:, f:f + 1], None, ALU.mult))
                emit_round(em, "vector", pf_a[:, :], pf_t[:, :], qscr["i1"][:, :],
                           wf[0][:, :], qscr["f2"][:, :])
                em.op("vector", lambda e: e.tensor_scalar(pf_t[:, :], pf_t[:, :],
                                                          127.0, -127.0, ALU.min, ALU.max))
                em.op("vector", lambda e, f=fb: e.tensor_copy(kT[f][:, :], pf_t[:, :]), inc=v)
            em.wait("sync", v)
            em.dma("sync", D["cks"][0:128, :], kscale[:, 0:1], st)
            em.dma("sync", D["cks"][128:256, :], kscale[:, 1:2], st)
            em.wait("scalar", v)
            em.wait("scalar", st, st.n)
            em.wait("tensor", v)
            for dj in range(16):
                em.wait("scalar", pe, pe.n)
                for fb in range(2):
                    em.op("scalar", lambda e, d=dj, f=fb: e.copy(
                        kqb[:, 128 * f:128 * (f + 1)], kT[f][:, 128 * d:128 * (d + 1)]), inc=sc)
                em.wait("tensor", sc)
                for fb in range(2):
                    em.op("tensor", lambda e, f=fb: e.transpose(
                        ps_t[f][:, 0:128], kqb[:, 128 * f:128 * (f + 1)], id_sb[:, :]),
                        inc=(pe if fb == 1 else None))
                em.wait("scalar", pe)
                for fb in range(2):
                    em.op("scalar", lambda e, d=dj, f=fb: e.copy(
                        stage[:, 256 * d + 128 * f:256 * d + 128 * (f + 1)], ps_t[f][:, 0:128]), inc=sc)
            store_stage_rows(1)

            for wname, iw in (("wq1_own", 2), ("wk1_own", 3), ("wv1_own", 4), ("wo1_own", 5)):
                em.wait("scalar", st, st.n)
                for oc in range(2):
                    em.wait("gpsimd", pe, pe.n)
                    em.dma("gpsimd", wbf[:, :], D[wname][128 * oc:128 * (oc + 1), :], gld)
                    em.wait("tensor", gld)
                    for cj in range(16):
                        par = cj % 2
                        em.wait("tensor", sc, sc.n - 1)
                        em.op("tensor", lambda e, c=cj, p=par: e.transpose(
                            ps_t[p][:, 0:128], wbf[:, 128 * c:128 * (c + 1)], id_sb[:, :]), inc=pe)
                        em.wait("scalar", pe)
                        em.op("scalar", lambda e, c=cj, o=oc, p=par: e.copy(
                            stage[:, 256 * c + 128 * o:256 * c + 128 * (o + 1)], ps_t[p][:, 0:128]), inc=sc)
                store_stage_rows(iw)

            em.wait("gpsimd", st, st.n)
            em.op("gpsimd", lambda e: e.collective_compute(
                "AllGather", ALU.bypass, ALLC, [h_cprep.ap().opt()], [h_gprep.ap().opt()]), inc=cc)
            em.op("gpsimd", lambda e: e.collective_compute(
                "AllGather", ALU.bypass, ALLC, [h_cks.ap().opt()], [h_gks.ap().opt()]), inc=cc)
            marks["cc_prep"] = cc.n
            esP.close()

        def stage_A(layer):
            ngr = 4 if layer == 0 else 16
            vew = 130 * ngr
            esA = ExitStack()
            sbA = lambda n, sh, dt=F32: esA.enter_context(nc.sbuf_tensor(n, sh, dt))
            psA = lambda n, sh, dt=F32: esA.enter_context(nc.psum_tensor(n, sh, dt))
            L = f"a{layer}_"
            gsb = sbA(L + "gsb", [128, E])
            xnf = [sbA(L + f"xnf{i}", [128, E]) for i in range(2)]
            xq = [sbA(L + f"xq{i}", [128, E], BF16) for i in range(2)]
            xqT = [sbA(L + f"xqT{i}", [128, 16 * 128], BF16) for i in range(2)]
            stage = [sbA(L + f"stage{i}", [128, 1024], BF16) for i in range(2)]
            cols = sbA(L + "cols", [128, 8])
            cols2 = sbA(L + "cols2", [128, 4])
            stats = sbA(L + "stats", [128, 32])
            srow = sbA(L + "srow", [1, 8])
            ffnb_bf = sbA(L + "ffnb", [1, E], BF16)
            rows = sbA(L + "rows", [1, 4 * E])
            i1t = sbA(L + "i1t", [128, E], I32)
            f2b = sbA(L + "f2b", [128, E], BF16)
            vstage = sbA(L + "vstage", [128, 520], BF16)
            ztile = sbA(L + "ztile", [128, vew], BF16)
            if layer == 0:
                QA = sbA(L + "QA", [128, E])
                QT = sbA(L + "QT", [128, E])
                qscr = {"af": QA, "tf": QT, "i1": i1t, "f2": f2b,
                        "a2": sbA(L + "qa2", [128, E], BF16),
                        "c2": sbA(L + "qc2", [128, E], BF16),
                        "c4": sbA(L + "qc4", [128, E], BF16)}
                bamax = sbA(L + "bamax", [128, 128])
                bsq = sbA(L + "bsq", [128, 128])
                invd = sbA(L + "invd", [128, 128])
                e4scr = {"f1": sbA(L + "e41", [128, 128]), "f2": sbA(L + "e42", [128, 128]),
                         "f3": sbA(L + "e43", [128, 128]), "i1": sbA(L + "e4i", [128, 128], I32)}
                xtbuf = [gsb, QA]
            else:
                ysb = sbA(L + "ysb", [128, E])
                skb = sbA(L + "skb", [128, E])
                bqb = sbA(L + "bqb", [128, E])
                qf1 = sbA(L + "qf1", [128, E])
                xtbuf = [gsb, ysb]

            ps_mm = [psA(L + f"mm{i}", [128, 512]) for i in range(2)]
            ps_tr = [psA(L + f"tr{i}", [128, 1024], BF16) for i in range(2)]
            ps_bc = psA(L + "bc", [128, 512])

            ld_rows = em.sem(L + "ld_rows")
            ldx = [em.sem(L + "ldx0"), em.sem(L + "ldx1")]
            ld_r = [em.sem(L + "ld_r0"), em.sem(L + "ld_r1")]
            ld_rq = [em.sem(L + "ld_rq0"), em.sem(L + "ld_rq1")]
            ld_wf = em.sem(L + "ld_wf")
            ld_wf2 = em.sem(L + "ld_wf2")
            ld_xnf = [em.sem(L + "ld_xnf0"), em.sem(L + "ld_xnf1")]

            xsrc = D["x_own"] if layer == 0 else D["x1_d"]
            if layer == 1:
                em.wait("sync", st_o[0], marks["x1_st0"])
                em.wait("sync", st_o[1], marks["x1_st1"])
                em.wait("gpsimd", st_o[0], marks["x1_st0"])
                em.wait("gpsimd", st_o[1], marks["x1_st1"])

            if layer == 0:
                em.dma("sync", rows[0:1, 0:E], D["bq_row"][:, :], ld_rows)
                em.dma("sync", rows[0:1, E:E + 512], D["bk_row"][:, :], ld_rows)
                em.dma("sync", rows[0:1, E + 512:E + 1024], D["bv_row"][:, :], ld_rows)
                em.dma("sync", rows[0:1, 2 * E:3 * E], D["ffn_bias0"][:, :], ld_rows)
                em.wait("sync", st_w, marks["wts_store"])
                em.dma("sync", srow[0:1, 0:1], D["wtr"][0:1, 0:1], ld_rows)
                em.dma("sync", srow[0:1, 1:2], D["xts"][:, :], ld_rows)
            else:
                em.wait("sync", cc, marks["cc_prep"])
                em.dma("sync", rows[0:1, 0:E],
                       D["gks"].rearrange("(o f) z -> o (f z)", o=1), ld_rows)
                em.dma("sync", rows[0:1, E:2 * E], D["ffn_bias1"][:, :], ld_rows)
            rows_ld = ld_rows.n

            es2 = ExitStack()
            xnT = es2.enter_context(nc.sbuf_tensor(L + "xnT", [128, 16 * 1024], BF16))
            rows_bf = es2.enter_context(nc.sbuf_tensor(L + "rows_bf", [1, 2 * E], BF16))
            wR = [es2.enter_context(nc.sbuf_tensor(L + f"wR{i}", [128, 16 * 512], BF16)) for i in range(2)]

            xt_free = [(0, 0), (0, 0)]
            for i in range(8):
                par = i % 2
                em.wait("sync", v, xt_free[par][0])
                em.wait("sync", scs, xt_free[par][1])
                em.dma("sync", xtbuf[par][:, :], xsrc[128 * i:128 * (i + 1), :], ldx[par])
                em.wait("vector", ldx[par])
                em.op("vector", lambda e, p=par: e.tensor_reduce(cols[:, 0:1], xtbuf[p][:, :], AX.X, ALU.add))
                em.op("vector", lambda e: e.tensor_single_scalar(cols[:, 0:1], cols[:, 0:1], -1.0 / 2048.0, ALU.mult), inc=v)
                em.wait("scalar", ldx[par])
                em.wait("scalar", v)
                em.wait("scalar", pe, pe.n)
                em.op("scalar", lambda e, p=par: e.activation(xq[0][:, :], xtbuf[p][:, :], AF.Square,
                                                              accum_out=cols[:, 1:2]), inc=scs)
                em.wait("vector", scs)
                em.op("vector", lambda e: e.tensor_single_scalar(cols[:, 1:2], cols[:, 1:2], 1.0 / 2048.0, ALU.mult))
                em.op("vector", lambda e: e.tensor_tensor(cols[:, 2:3], cols[:, 0:1], cols[:, 0:1], ALU.mult))
                em.op("vector", lambda e: e.tensor_tensor(cols[:, 1:2], cols[:, 1:2], cols[:, 2:3], ALU.subtract), inc=v)
                em.wait("scalar", v)
                em.op("scalar", lambda e: e.activation(cols[:, 3:4], cols[:, 1:2], AF.Sqrt, bias=cst[:, 2:3]), inc=scs)
                em.wait("vector", scs)
                em.op("vector", lambda e, i_=i: e.reciprocal(stats[:, 2 * i_ + 1:2 * i_ + 2], cols[:, 3:4]))
                em.op("vector", lambda e, i_=i: e.tensor_copy(stats[:, 2 * i_:2 * i_ + 1], cols[:, 0:1]))
                em.op("vector", lambda e, i_=i, p=par: e.tensor_scalar(
                    xq[0][:, :], xtbuf[p][:, :],
                    stats[:, 2 * i_:2 * i_ + 1], stats[:, 2 * i_ + 1:2 * i_ + 2],
                    ALU.add, ALU.mult), inc=v)
                xt_free[par] = (v.n, scs.n)
                em.wait("tensor", v)
                for q4 in range(4):
                    pst = ps_tr[q4 % 2]
                    em.wait("tensor", evt, evt.n - 1)
                    for j4 in range(4):
                        j = 4 * q4 + j4
                        em.op("tensor", lambda e, jj=j, p=pst, o=j4: e.transpose(
                            p[:, 128 * o:128 * (o + 1)],
                            xq[0][:, 128 * jj:128 * (jj + 1)], id_sb[:, :]),
                            inc=(pe if j4 == 3 else None))
                    em.wait("scalar", pe)
                    em.op("scalar", lambda e, i_=i, q=q4, p=pst: e.copy(
                        xnT[:, :].rearrange("p (c t) -> p c t", t=1024)[:, 4 * q:4 * (q + 1), 128 * i_:128 * (i_ + 1)],
                        p[:, 0:512].rearrange("p (c o) -> p c o", o=128)), inc=evt)
            xnT_ready = evt.n
            ph1_pe = pe.n

            region_last_pe = [0, 0]

            def load_quarter_g(ridx, iw, qi, eng="sync"):
                sem = ld_r[ridx] if eng == "gpsimd" else ld_rq[ridx]
                em.wait(eng, pe, region_last_pe[ridx])
                em.wait(eng, cc, marks["cc_prep"])
                for ec in range(16):
                    for ri, rr in enumerate((2 * qi, 2 * qi + 1)):
                        em.dma(eng,
                               wR[ridx][:, 512 * ec + 256 * ri:512 * ec + 256 * (ri + 1)],
                               D["gprep"][rr * 6 * E + 2048 * iw + 128 * ec:rr * 6 * E + 2048 * iw + 128 * (ec + 1), :],
                               sem)
                return (sem, sem.n)

            def load_l0_wq_quarter(ridx, qi):
                em.wait("gpsimd", pe, region_last_pe[ridx])
                for ec in range(16):
                    em.dma("gpsimd",
                           wR[ridx][:, 512 * ec:512 * (ec + 1)].rearrange("p (h d) -> p h d", d=128),
                           D["wq"][128 * ec:128 * (ec + 1), 4 * qi:4 * (qi + 1), :], ld_r[ridx])
                return (ld_r[ridx], ld_r[ridx].n)

            def load_l0_kv(ridx, name):
                em.wait("gpsimd", pe, region_last_pe[ridx])
                for ec in range(16):
                    em.dma("gpsimd",
                           wR[ridx][:, 512 * ec:512 * (ec + 1)].rearrange("p (g d) -> p g d", d=128),
                           D[name][128 * ec:128 * (ec + 1), :, :], ld_r[ridx])
                return (ld_r[ridx], ld_r[ridx].n)

            def wait_mark(eng, mark):
                em.wait(eng, em.sem(mark[0]), mark[1])

            if layer == 0:
                em.wait("scalar", ld_rows, rows_ld)
                em.op("scalar", lambda e: e.copy(rows_bf[0:1, 0:E + 1024], rows[0:1, 0:E + 1024]), inc=scs)
            rows_bf_ready = scs.n
            em.op("vector", lambda e: e.memset(vstage[:, :], 0.0))
            em.op("vector", lambda e: e.memset(
                vstage[:, :].rearrange("t (h o) -> t h o", o=130)[:, :, 128:129], 1.0), inc=v)
            vst_ready = v.n

            em.wait("vector", ld_rows, rows_ld)
            if layer == 0:
                em.op("vector", lambda e: e.tensor_tensor(srow[0:1, 2:3], srow[0:1, 0:1], srow[0:1, 1:2], ALU.mult))
                em.op("vector", lambda e: e.tensor_copy(srow[0:1, 3:4], srow[0:1, 2:3]))
                em.op("vector", lambda e: e.reciprocal(srow[0:1, 3:4], srow[0:1, 3:4]))
                em.op("vector", lambda e: e.tensor_single_scalar(srow[0:1, 4:5], srow[0:1, 1:2], 6.0, ALU.mult))
                em.op("vector", lambda e: e.reciprocal(srow[0:1, 4:5], srow[0:1, 4:5]))
                em.op("vector", lambda e: e.tensor_copy(srow[0:1, 5:6], srow[0:1, 1:2]))
                em.op("vector", lambda e: e.reciprocal(srow[0:1, 5:6], srow[0:1, 5:6]))
                em.op("vector", lambda e: e.tensor_scalar(rows[0:1, 3 * E:4 * E], rows[0:1, 2 * E:3 * E],
                                                          srow[0:1, 3:4], None, ALU.mult), inc=v)
                em.wait("scalar", v)
                em.op("scalar", lambda e: e.copy(ffnb_bf[0:1, :], rows[0:1, 3 * E:4 * E]), inc=scs)
                ffnb_ready = scs.n
                em.wait("tensor", v)
                em.op("tensor", lambda e: e.matmul(ps_bc[:, 0:1], ones_f[0:1, 0:128], srow[0:1, 2:3], start=True, stop=True))
                em.op("tensor", lambda e: e.matmul(ps_bc[:, 1:2], ones_f[0:1, 0:128], srow[0:1, 4:5], start=True, stop=True))
                em.op("tensor", lambda e: e.matmul(ps_bc[:, 2:3], ones_f[0:1, 0:128], srow[0:1, 5:6], start=True, stop=True), inc=pe)
                em.wait("vector", pe)
                em.op("vector", lambda e: e.tensor_copy(cols2[:, 0:3], ps_bc[:, 0:3]), inc=v)
                bc_ready = v.n
            else:
                em.wait("tensor", ld_rows, rows_ld)
                for j in range(4):
                    if j > 0:
                        em.wait("tensor", v)
                    em.op("tensor", lambda e, j_=j: e.matmul(
                        ps_bc[:, 0:512], ones_f[0:1, 0:128], rows[0:1, 512 * j_:512 * (j_ + 1)],
                        start=True, stop=True), inc=pe)
                    em.wait("vector", pe)
                    em.op("vector", lambda e, j_=j: e.tensor_copy(skb[:, 512 * j_:512 * (j_ + 1)], ps_bc[:, 0:512]), inc=v)
                em.op("vector", lambda e: e.tensor_reduce(srow[0:1, 0:1], rows[0:1, E:2 * E], AX.X,
                                                          ALU.max, apply_absolute_value=True))
                emit_int8_cols(em, srow[0:1, 0:1], srow[0:1, 1:2], srow[0:1, 2:3],
                               srow[0:1, 3:4], srow[0:1, 4:5])
                em.op("vector", lambda e: e.tensor_scalar(rows[0:1, 2 * E:3 * E], rows[0:1, E:2 * E],
                                                          srow[0:1, 1:2], None, ALU.mult))
                emit_round(em, "vector", rows[0:1, 2 * E:3 * E], rows[0:1, 2 * E:3 * E],
                           i1t[0:1, 0:E], gsb[0:1, 0:E], ysb[0:1, 0:E])
                em.op("vector", lambda e: e.tensor_scalar(rows[0:1, 2 * E:3 * E], rows[0:1, 2 * E:3 * E],
                                                          127.0, -127.0, ALU.min, ALU.max))
                em.op("vector", lambda e: e.tensor_scalar(rows[0:1, 2 * E:3 * E], rows[0:1, 2 * E:3 * E],
                                                          srow[0:1, 2:3], None, ALU.mult), inc=v)
                for j in range(4):
                    em.wait("tensor", v)
                    em.op("tensor", lambda e, j_=j: e.matmul(
                        ps_bc[:, 0:512], ones_f[0:1, 0:128], rows[0:1, 2 * E + 512 * j_:2 * E + 512 * (j_ + 1)],
                        start=True, stop=True), inc=pe)
                    em.wait("vector", pe)
                    em.op("vector", lambda e, j_=j: e.tensor_copy(bqb[:, 512 * j_:512 * (j_ + 1)], ps_bc[:, 0:512]), inc=v)
                bc_ready = v.n
                ffnb_ready = scs.n


            ev_free = [0, 0]
            st_free = [0, 0]
            grp = [0]
            spar_ctr = [0]

            def mm_group(lhs_fn, rhs_fn, n_k, bias_lhs_rhs, evac_fn):
                p2 = grp[0] % 2
                grp[0] += 1
                em.wait("tensor", evq, ev_free[p2])
                for kk in range(n_k):
                    last = (kk == n_k - 1) and bias_lhs_rhs is None
                    em.op("tensor", lambda e, k=kk, p=p2, l=last: e.matmul(
                        ps_mm[p][:, 0:512], lhs_fn(k), rhs_fn(k),
                        start=(k == 0), stop=l), inc=(pe if last else None))
                if bias_lhs_rhs is not None:
                    bl, br = bias_lhs_rhs
                    em.op("tensor", lambda e, p=p2: e.matmul(
                        ps_mm[p][:, 0:512], bl, br, start=False, stop=True), inc=pe)
                em.wait("scalar", pe)
                evac_fn(p2)
                ev_free[p2] = evq.n

            ckv = D["ckv%d" % layer]
            kw_ = 128 * ngr

            def ck_dst(gc):
                return ckv[0:1024, 0:kw_].rearrange("(j p) (g k) -> p j g k", p=128, k=128)[:, :, gc, :]

            def proj_fm(ridx, mark, kind, oc0, noc, bias_off):
                wait_mark("tensor", mark)
                em.wait("tensor", evt, xnT_ready)
                if bias_off is not None:
                    em.wait("tensor", scs, rows_bf_ready)
                for oc in range(noc):
                    abs_oc = oc0 + oc
                    spar = spar_ctr[0] % 2
                    spar_ctr[0] += 1
                    em.wait("scalar", st_s[spar], st_free[spar])
                    for tb in range(2):
                        bias = None
                        if bias_off is not None:
                            bias = (rows_bf[0:1, bias_off + 128 * abs_oc:bias_off + 128 * (abs_oc + 1)],
                                    ones_bf[0:1, 0:512])
                        mm_group(
                            lambda k, oc_=oc, r=ridx: wR[r][:, 512 * k + 128 * oc_:512 * k + 128 * (oc_ + 1)],
                            lambda k, t=tb: xnT[:, 1024 * k + 512 * t:1024 * k + 512 * (t + 1)],
                            16, bias,
                            lambda p2, t=tb, sp=spar: em.op("scalar", lambda e, p=p2, t_=t, s_=sp: e.copy(
                                stage[s_][:, 512 * t_:512 * (t_ + 1)], ps_mm[p][:, 0:512]), inc=evq))
                    em.wait("sync", evq)
                    if kind == "q":
                        em.dma("sync", D["qfm_d%d" % layer][128 * abs_oc:128 * (abs_oc + 1), :],
                               stage[spar][:, :], st_s[spar])
                    else:
                        em.dma("sync", ck_dst(abs_oc),
                               stage[spar][:, :].rearrange("p (j k) -> p j k", k=128), st_s[spar])
                    st_free[spar] = st_s[spar].n
                region_last_pe[ridx] = pe.n

            st_v = em.sem(L + "st_v")

            def proj_tm(ridx, mark, ob, bias_off):
                wait_mark("tensor", mark)
                em.wait("tensor", evt, xnT_ready)
                if bias_off is not None:
                    em.wait("tensor", scs, rows_bf_ready)
                em.wait("scalar", v, vst_ready)
                for i in range(8):
                    em.wait("scalar", st_v, st_v.n)
                    bias = None
                    if bias_off is not None:
                        bias = (ones_bf[0:1, 0:128], rows_bf[0:1, bias_off:bias_off + 512])
                    mm_group(
                        lambda k, i_=i: xnT[:, 1024 * k + 128 * i_:1024 * k + 128 * (i_ + 1)],
                        lambda k, r=ridx: wR[r][:, 512 * k:512 * (k + 1)],
                        16, bias,
                        lambda p2: em.op("scalar", lambda e, p=p2: e.copy(
                            vstage[:, :].rearrange("t (h o) -> t h o", o=130)[:, :, 0:128],
                            ps_mm[p][:, 0:512].rearrange("t (h o) -> t h o", o=128)), inc=evq))
                    em.wait("sync", evq)
                    em.dma("sync",
                           ckv[1024 + 128 * i:1024 + 128 * (i + 1), 520 * ob:520 * (ob + 1)],
                           vstage[:, :], st_v)
                region_last_pe[ridx] = pe.n

            def emit_kv_collective():
                em.op("vector", lambda e: e.memset(ztile[:, :], 0.0), inc=v)
                em.wait("sync", v)
                em.dma("sync", ckv[2048:2176, 0:vew], ztile[:, :], st_g)
                jw = vew - kw_
                em.dma("sync", ckv[0:1024, kw_:vew].rearrange("(j p) c -> p j c", p=128),
                       ztile[:, 0:8 * jw].rearrange("p (j c) -> p j c", c=jw), st_g)
                em.wait("gpsimd", st_s[0], st_s[0].n)
                em.wait("gpsimd", st_s[1], st_s[1].n)
                em.wait("gpsimd", st_v, st_v.n)
                em.wait("gpsimd", st_g, st_g.n)
                em.op("gpsimd", lambda e, ly=layer: e.collective_compute(
                    "AllGather", ALU.bypass, ALLC, [h_ckv[ly].ap().opt()], [h_gkv[ly].ap().opt()]), inc=cc)
                marks["cc_kv%d" % layer] = cc.n

            if layer == 0:
                mk = load_l0_kv(0, "wk")
                mv = load_l0_kv(1, "wv")
                proj_fm(0, mk, "k", 0, 4, E)
                m0 = load_l0_wq_quarter(0, 0)
                proj_tm(1, mv, 0, E + 512)
                m1 = load_l0_wq_quarter(1, 1)
                emit_kv_collective()
                proj_fm(0, m0, "q", 0, 4, 0)
                m2 = load_l0_wq_quarter(0, 2)
                proj_fm(1, m1, "q", 4, 4, 0)
                m3 = load_l0_wq_quarter(1, 3)
                proj_fm(0, m2, "q", 8, 4, 0)
                proj_fm(1, m3, "q", 12, 4, 0)
            else:
                mk0 = load_quarter_g(0, 3, 0, "gpsimd")
                mk1 = load_quarter_g(1, 3, 1, "gpsimd")
                proj_fm(0, mk0, "k", 0, 4, None)
                mk2 = load_quarter_g(0, 3, 2, "gpsimd")
                proj_fm(1, mk1, "k", 4, 4, None)
                mk3 = load_quarter_g(1, 3, 3, "gpsimd")
                proj_fm(0, mk2, "k", 8, 4, None)
                mv0 = load_quarter_g(0, 4, 0, "gpsimd")
                proj_fm(1, mk3, "k", 12, 4, None)
                mv1 = load_quarter_g(1, 4, 1, "gpsimd")
                proj_tm(0, mv0, 0, None)
                mv2 = load_quarter_g(0, 4, 2, "gpsimd")
                proj_tm(1, mv1, 1, None)
                mv3 = load_quarter_g(1, 4, 3, "gpsimd")
                proj_tm(0, mv2, 2, None)
                mq0 = load_quarter_g(0, 2, 0, "gpsimd")
                proj_tm(1, mv3, 3, None)
                mq1 = load_quarter_g(1, 2, 1, "gpsimd")
                emit_kv_collective()
                proj_fm(0, mq0, "q", 0, 4, None)
                mq2 = load_quarter_g(0, 2, 2, "gpsimd")
                proj_fm(1, mq1, "q", 4, 4, None)
                mq3 = load_quarter_g(1, 2, 3, "gpsimd")
                proj_fm(0, mq2, "q", 8, 4, None)
                proj_fm(1, mq3, "q", 12, 4, None)

            qkv_pe, qkv_v, qkv_scs = pe.n, v.n, scs.n
            es2.close()
            wF = esA.enter_context(nc.sbuf_tensor(L + "wF", [128, 16 * 2048], BF16))
            iw_f = 0 if layer == 0 else 1
            def emit_wf_loads():
                for eng in ("sync", "gpsimd"):
                    em.wait(eng, pe, qkv_pe)
                    em.wait(eng, v, qkv_v)
                    em.wait(eng, scs, qkv_scs)
                    em.wait(eng, cc, marks["cc_prep"])
                for ec in range(16):
                    for rr in range(8):
                        eng, sem = ("sync", ld_wf) if rr < 4 else ("gpsimd", ld_wf2)
                        em.dma(eng, wF[:, 2048 * ec + 256 * rr:2048 * ec + 256 * (rr + 1)],
                               D["gprep"][rr * 6 * E + 2048 * iw_f + 128 * ec:rr * 6 * E + 2048 * iw_f + 128 * (ec + 1), :],
                               sem)
                return (ld_wf.n, ld_wf2.n)


            xnf_free = [0, 0]
            xq_pe_free = [ph1_pe, ph1_pe]
            qmark = {}
            post_mark = [0]

            def ffn_quant(i):
                par = i % 2
                em.wait("sync", v, xnf_free[par])
                em.dma("sync", xnf[par][:, :], xsrc[128 * i:128 * (i + 1), :], ld_xnf[par])
                em.wait("vector", ld_xnf[par])
                em.wait("vector", v, bc_ready)
                em.wait("vector", pe, xq_pe_free[par])
                em.op("vector", lambda e, i_=i, p=par: e.tensor_scalar(
                    xnf[p][:, :], xnf[p][:, :],
                    stats[:, 2 * i_:2 * i_ + 1], stats[:, 2 * i_ + 1:2 * i_ + 2],
                    ALU.add, ALU.mult))
                if layer == 0:
                    em.op("vector", lambda e, p=par: e.tensor_reduce(
                        bamax[:, :].rearrange("p (b k) -> p b k", k=16),
                        xnf[p][:, :].rearrange("p (b k) -> p b k", k=16),
                        AX.X, ALU.max, apply_absolute_value=True))
                    emit_e4m3(em, bamax[:, :], cols2[:, 1:2], cols2[:, 2:3], bsq, invd, e4scr, sv, ss,
                              cst[:, 0:1], cst[:, 1:2])
                    emit_e2m1(em, xnf[par][:, :], invd, bsq, xq[par][:, :], qscr, sv, ss, inc=v)
                else:
                    em.op("vector", lambda e, p=par: e.tensor_reduce(
                        cols[:, 0:1], xnf[p][:, :], AX.X, ALU.max, apply_absolute_value=True))
                    emit_int8_cols(em, cols[:, 0:1], cols[:, 1:2], stats[:, 16 + i:17 + i],
                                   cols[:, 2:3], cols[:, 3:4])
                    em.op("vector", lambda e, p=par: e.tensor_scalar(
                        xnf[p][:, :], xnf[p][:, :], cols[:, 1:2], None, ALU.mult))
                    emit_round(em, "vector", xnf[par][:, :], xq[par][:, :], i1t[:, :],
                               qf1[:, :], f2b[:, :])
                    em.op("vector", lambda e, p=par: e.tensor_scalar(
                        xq[p][:, :], xq[p][:, :], 127.0, -127.0, ALU.min, ALU.max), inc=v)
                qmark[i] = v.n
                xnf_free[par] = v.n

            def ffn_mm(i):
                par = i % 2
                em.wait("tensor", v, qmark[i])
                for q4 in range(4):
                    pst = ps_tr[q4 % 2]
                    em.wait("tensor", evt, evt.n - 1)
                    for j4 in range(4):
                        jq = 4 * q4 + j4
                        em.op("tensor", lambda e, jj=jq, p=pst, o=j4, pr=par: e.transpose(
                            p[:, 128 * o:128 * (o + 1)], xq[pr][:, 128 * jj:128 * (jj + 1)], id_sb[:, :]),
                            inc=(pe if j4 == 3 else None))
                    em.wait("scalar", pe)
                    em.op("scalar", lambda e, q=q4, p=pst, pr=par: e.copy(
                        xqT[pr][:, 512 * q:512 * (q + 1)], p[:, 0:512]), inc=evt)
                em.wait("tensor", evt)
                xq_pe_free[par] = pe.n
                em.wait("tensor", ld_wf, wf_mark[0])
                em.wait("tensor", ld_wf2, wf_mark[1])
                if layer == 0:
                    em.wait("tensor", scs, ffnb_ready)
                else:
                    em.wait("scalar", v, post_mark[0])
                for nb in range(4):
                    bias = None
                    if layer == 0:
                        bias = (ones_bf[0:1, 0:128], ffnb_bf[0:1, 512 * nb:512 * (nb + 1)])
                    if layer == 0:
                        evac = (lambda p2, c=512 * nb: em.op("scalar", lambda e, p=p2, c_=c: e.activation(
                            gsb[:, c_:c_ + 512], ps_mm[p][:, 0:512], AF.Silu,
                            scale=cols2[:, 0:1]), inc=evq))
                    else:
                        evac = (lambda p2, c=512 * nb, i_=i: em.op("scalar", lambda e, p=p2, c_=c, ii=i_: e.activation(
                            ysb[:, c_:c_ + 512], ps_mm[p][:, 0:512], AF.Copy,
                            scale=stats[:, 16 + ii:17 + ii]), inc=evq))
                    mm_group(
                        lambda k, pr=par: xqT[pr][:, 128 * k:128 * (k + 1)],
                        lambda k, n=nb: wF[:, 2048 * k + 512 * n:2048 * k + 512 * (n + 1)],
                        16, bias, evac)
                if layer == 0:
                    em.wait("sync", evq)
                    em.dma("sync", D["g_d0"][128 * i:128 * (i + 1), :], gsb[:, :], st_g)
                    em.wait("scalar", st_g)
                else:
                    em.wait("vector", evq)
                    em.wait("vector", st_g, st_g.n)
                    em.op("vector", lambda e: e.tensor_tensor(ysb[:, :], ysb[:, :], skb[:, :], ALU.mult))
                    em.op("vector", lambda e: e.tensor_tensor(ysb[:, :], ysb[:, :], bqb[:, :], ALU.add))
                    em.op("vector", lambda e: e.tensor_single_scalar(ysb[:, :], ysb[:, :], 0.0, ALU.max))
                    em.op("vector", lambda e: e.tensor_reduce(cols[:, 0:1], ysb[:, :], AX.X,
                                                              ALU.max, apply_absolute_value=True))
                    emit_int8_cols(em, cols[:, 0:1], cols[:, 1:2], cols[:, 4:5],
                                   cols[:, 2:3], cols[:, 3:4])
                    em.op("vector", lambda e: e.tensor_scalar(gsb[:, :], ysb[:, :], cols[:, 1:2], None, ALU.mult))
                    emit_round(em, "vector", gsb[:, :], xq[par][:, :], i1t[:, :],
                               ysb[:, :], f2b[:, :])
                    em.op("vector", lambda e, p=par: e.tensor_scalar(
                        xq[p][:, :], xq[p][:, :], 127.0, -127.0, ALU.min, ALU.max))
                    em.op("vector", lambda e, p=par: e.tensor_scalar(
                        ysb[:, :], xq[p][:, :], cols[:, 4:5], None, ALU.mult), inc=v)
                    post_mark[0] = v.n
                    em.wait("scalar", v)
                    em.op("scalar", lambda e: e.activation(gsb[:, :], ysb[:, :], AF.Silu), inc=scs)
                    em.wait("sync", scs)
                    em.dma("sync", D["g_d1"][128 * i:128 * (i + 1), :], gsb[:, :], st_g)
                    em.wait("scalar", st_g)

            PIPE_FFN = True
            if PIPE_FFN:
                ffn_quant(0)
                ffn_quant(1)
                wf_mark = emit_wf_loads()
                ffn_mm(0)
                for i in range(2, 8):
                    ffn_quant(i)
                    ffn_mm(i - 1)
                ffn_mm(7)
            else:
                wf_mark = emit_wf_loads()
                for i in range(8):
                    ffn_quant(i)
                    ffn_mm(i)
            marks["g_st%d" % layer] = st_g.n
            esA.close()

        def stage_B(layer):
            ngr = 4 if layer == 0 else 16
            vew = 130 * ngr
            esB = ExitStack()
            sbB = lambda n, sh, dt=F32: esB.enter_context(nc.sbuf_tensor(n, sh, dt))
            L = f"b{layer}_"
            qsb = sbB(L + "qsb", [128, 16 * 1024], BF16)
            ksb = [sbB(L + f"ksb{i}", [128, vew], BF16) for i in range(2)]
            vsb = [sbB(L + f"vsb{i}", [128, vew], BF16) for i in range(2)]
            ptb = [sbB(L + f"ptb{i}", [128, 2048], BF16) for i in range(2)]
            lsb = sbB(L + "lsb", [1, 2048])
            oTs = sbB(L + "oTs", [128, 2048], F32 if layer == 0 else BF16)
            ssb = sbB(L + "ssb", [128, E])
            xg = [sbB(L + f"xg{i}", [128, E]) for i in range(2)]
            ost = sbB(L + "ost", [128, E])
            kidx_sb = sbB(L + "kidx", [128, NKT], I32)
            vidx_sb = sbB(L + "vidx", [128, NKT], I32)
            if layer == 0:
                lrq = sbB(L + "lrq", [128, 16])
            else:
                sTn = sbB(L + "sTn", [128, 8 * 2048], BF16)
                wo_sb = sbB(L + "wo", [128, 16 * 2048], BF16)

            ld2 = em.sem(L + "ld2")
            ldg = em.sem(L + "ldg")
            wol = em.sem(L + "wol")
            ldix = em.sem(L + "ldix")
            pes = em.sem(L + "pes")
            exd = em.sem(L + "exd")
            mkd = em.sem(L + "mkd")
            lmm = em.sem(L + "lmm")
            lev = em.sem(L + "lev")
            ltr = em.sem(L + "ltr")
            stx = em.sem(L + "stx")
            oev = em.sem(L + "oev")
            lrc = em.sem(L + "lrc")
            bcd = em.sem(L + "bcd")
            cmb = em.sem(L + "cmb")
            v2 = em.sem(L + "v2")

            out_d = D["x1_d"] if layer == 0 else D["xo"]
            out_sems = st_o if layer == 0 else [em.sem("b1_sto0"), em.sem("b1_sto1")]
            xsrc = D["x_own"] if layer == 0 else D["x1_d"]
            gsrc = D["g_d%d" % layer]
            gkv = D["gkv%d" % layer]

            em.dma("sync", kidx_sb[:, :], D["kidx%d" % layer][:, :], ldix)
            em.dma("sync", vidx_sb[:, :], D["vidx%d" % layer][:, :], ldix)
            em.wait("sync", st_s[0], st_s[0].n)
            em.wait("sync", st_s[1], st_s[1].n)
            for h in range(16):
                em.dma("sync",
                       qsb[:, :].rearrange("p (j hq) -> p j hq", hq=2048)[:, :, 128 * h:128 * (h + 1)],
                       D["qfm_d%d" % layer][128 * h:128 * (h + 1), :].rearrange("p (j q) -> p j q", q=128),
                       ld2)
            if layer == 1:
                em.wait("sync", cc, marks["cc_prep"])
                for ec in range(16):
                    for rr in range(8):
                        em.dma("sync", wo_sb[:, 2048 * ec + 256 * rr:2048 * ec + 256 * (rr + 1)],
                               D["gprep"][rr * 6 * E + 2048 * 5 + 128 * ec:rr * 6 * E + 2048 * 5 + 128 * (ec + 1), :],
                               wol)
            q_ready = ld2.n
            em.wait("gpsimd", ldix)
            em.wait("gpsimd", cc, marks["cc_kv%d" % layer])

            with ExitStack() as ps_es:
                ps_sc = [ps_es.enter_context(nc.psum_tensor(L + f"sc{g}", [128, 512], F32))
                         for g in range(4)]
                ps_o = [ps_es.enter_context(nc.psum_tensor(L + f"po{g}", [128, 512], F32))
                        for g in range(4)]

                kv_free = [0, 0]
                av_hist = [0, 0]
                sc_free = 0
                oev_free = 0
                lrc_free = 0
                ots_free = 0
                for j, (koff, Lk) in enumerate(slot_offsets()):
                    qbase = 2048 * j
                    exd0 = exd.n
                    lev0 = lev.n
                    for kt in range(Lk):
                        par = kt % 2
                        kta = koff + kt
                        em.wait("gpsimd", lmm, kv_free[par])
                        em.op("gpsimd", lambda e, p=par, a=kta: e.indirect_dma_start(
                            out=ksb[p][:, :], out_offset=None, in_=gkv[:, :],
                            in_offset=IndirectOffsetOnAxis(kidx_sb[:, a:a + 1], 0)),
                            inc=ldk[par], k=16)
                        em.op("gpsimd", lambda e, p=par, a=kta: e.indirect_dma_start(
                            out=vsb[p][:, :], out_offset=None, in_=gkv[:, :],
                            in_offset=IndirectOffsetOnAxis(vidx_sb[:, a:a + 1], 0)),
                            inc=ldk[par], k=16)
                        pb = ptb[par]
                        em.wait("tensor", ldk[par])
                        if j == 0 and kt == 0:
                            em.wait("tensor", ld2, q_ready)
                        if kt == 0:
                            em.wait("tensor", cmb, sc_free)
                        else:
                            em.wait("tensor", exd, exd0 + 4 * kt)
                        pes0 = pes.n
                        if layer == 0:
                            for g in range(4):
                                if kt > 0:
                                    em.wait("tensor", lev, lev0 + 4 * (kt - 1) + g + 1)
                                em.op("tensor", lambda e, g_=g, p=par, qb=qbase: e.matmul(
                                    ps_sc[g_][:, 0:512],
                                    ksb[p][:, 128 * g_:128 * (g_ + 1)],
                                    qsb[:, qb + 512 * g_:qb + 512 * (g_ + 1)],
                                    start=True, stop=True), inc=pes)
                        else:
                            for h in range(16):
                                if kt > 0 and h % 4 == 0:
                                    em.wait("tensor", lev, lev0 + 4 * (kt - 1) + h // 4 + 1)
                                em.op("tensor", lambda e, h_=h, p=par, qb=qbase: e.matmul(
                                    ps_sc[h_ // 4][:, 128 * (h_ % 4):128 * (h_ % 4) + 128],
                                    ksb[p][:, 128 * h_:128 * (h_ + 1)],
                                    qsb[:, qb + 128 * h_:qb + 128 * (h_ + 1)],
                                    start=(h_ % 4 == 0), stop=(h_ % 4 == 3)), inc=pes)
                        gstep = 1 if layer == 0 else 4
                        em.wait("scalar", lmm, av_hist[0])
                        exd_k = []
                        mkd_k = []
                        for g in range(4):
                            em.wait("scalar", pes, pes0 + gstep * (g + 1))
                            if layer == 0:
                                em.op("scalar", lambda e, g_=g, b=pb: e.activation(
                                    b[:, 512 * g_:512 * (g_ + 1)], ps_sc[g_][:, 0:512],
                                    AF.Exp, scale=RSQ), inc=exd)
                            else:
                                em.op("scalar", lambda e, g_=g, b=pb: e.activation(
                                    b[:, 512 * g_:512 * (g_ + 1)], ps_sc[g_][:, 0:512],
                                    AF.Relu, bias=cst[:, 3:4], scale=RSQ), inc=exd)
                            exd_k.append(exd.n)
                            if kt == 0:
                                em.wait("vector", exd, exd.n)
                                em.op("vector", lambda e, g_=g, b=pb: e.tensor_tensor(
                                    b[:, 512 * g_:512 * (g_ + 1)].rearrange("p (h q) -> p h q", q=128),
                                    b[:, 512 * g_:512 * (g_ + 1)].rearrange("p (h q) -> p h q", q=128),
                                    msk[:, :].rearrange("p (o q) -> p o q", o=1).broadcast_to([128, 4, 128]),
                                    ALU.mult), inc=mkd)
                                mkd_k.append(mkd.n)
                        if kt == 0:
                            em.wait("tensor", oev, oev_free)
                            em.wait("tensor", lrc, lrc_free)
                        lmm_k = []
                        for g in range(4):
                            if kt == 0:
                                em.wait("tensor", mkd, mkd_k[g])
                            else:
                                em.wait("tensor", exd, exd_k[g])
                            if layer == 0:
                                em.op("tensor", lambda e, g_=g, p=par, k=kt, l_=Lk, b=pb: e.matmul(
                                    ps_o[g_][:, 0:512],
                                    vsb[p][:, 130 * g_:130 * g_ + 128],
                                    b[:, 512 * g_:512 * (g_ + 1)],
                                    start=(k == 0), stop=(k == l_ - 1)))
                            else:
                                for h4 in range(4):
                                    h = 4 * g + h4
                                    em.op("tensor", lambda e, h_=h, g_=g, f=h4, p=par, k=kt, l_=Lk, b=pb: e.matmul(
                                        ps_o[g_][:, 128 * f:128 * (f + 1)],
                                        vsb[p][:, 130 * h_:130 * h_ + 128],
                                        b[:, 128 * h_:128 * (h_ + 1)],
                                        start=(k == 0 and f == 0), stop=(k == l_ - 1 and f == 3)))
                            em.op("tensor", lambda e, g_=g, p=par, b=pb: e.matmul(
                                ps_sc[g_][0:1, 0:512],
                                vsb[p][:, 128:129],
                                b[:, 512 * g_:512 * (g_ + 1)],
                                start=True, stop=True), inc=lmm)
                            lmm_k.append(lmm.n)
                        kv_free[par] = lmm.n
                        av_hist = [av_hist[1], lmm.n]
                        if kt == 0:
                            if layer == 0:
                                em.wait("vector", ltr, ltr.n)
                            else:
                                em.wait("vector", bcd, bcd.n)
                        for g in range(4):
                            em.wait("vector", lmm, lmm_k[g])
                            if kt == 0:
                                em.op("vector", lambda e, g_=g: e.tensor_copy(
                                    lsb[0:1, 512 * g_:512 * (g_ + 1)], ps_sc[g_][0:1, 0:512]),
                                    inc=lev)
                            else:
                                em.op("vector", lambda e, g_=g: e.tensor_tensor(
                                    lsb[0:1, 512 * g_:512 * (g_ + 1)], lsb[0:1, 512 * g_:512 * (g_ + 1)],
                                    ps_sc[g_][0:1, 0:512], ALU.add),
                                    inc=lev)
                    em.wait("scalar", lmm)
                    em.wait("scalar", stx if layer == 0 else cmb, ots_free)
                    for g in range(4):
                        em.op("scalar", lambda e, g_=g: e.copy(
                            oTs[:, 512 * g_:512 * (g_ + 1)], ps_o[g_][:, 0:512]), inc=oev)
                    if layer == 0:
                        em.wait("tensor", lev)
                        em.wait("tensor", exd)
                        em.wait("tensor", oev)
                        for h in range(16):
                            em.op("tensor", lambda e, h_=h: e.transpose(
                                ps_o[0][:, h_:h_ + 1], lsb[0:1, 128 * h_:128 * (h_ + 1)],
                                id_f[0:1, 0:1]), inc=(ltr if h == 15 else None))
                        for h in range(16):
                            em.op("tensor", lambda e, h_=h: e.transpose(
                                ps_sc[h_ // 4][:, 128 * (h_ % 4):128 * (h_ % 4) + 128],
                                oTs[:, 128 * h_:128 * (h_ + 1)], id_f[:, :]),
                                inc=(stx if h == 15 else None))
                        em.wait("vector", ltr)
                        em.op("vector", lambda e: e.reciprocal(lrq[:, :], ps_o[0][:, 0:16]), inc=lrc)
                        em.wait("vector", stx)
                        for h in range(16):
                            em.op("vector", lambda e, h_=h: e.tensor_scalar(
                                ssb[:, 128 * h_:128 * (h_ + 1)],
                                ps_sc[h_ // 4][:, 128 * (h_ % 4):128 * (h_ % 4) + 128],
                                lrq[:, h_:h_ + 1], None, ALU.mult),
                                inc=(cmb if h == 15 else None))
                        em.wait("sync", v2, v2.n)
                        em.wait("sync", st_g, marks["g_st0"])
                        em.dma("sync", xg[0][:, :], xsrc[128 * j:128 * (j + 1), :], ldg)
                        em.dma("sync", xg[1][:, :], gsrc[128 * j:128 * (j + 1), :], ldg)
                        em.wait("vector", ldg)
                        em.wait("vector", out_sems[0], 16 * j)
                        em.op("vector", lambda e: e.tensor_tensor(ost[:, :], ssb[:, :], xg[1][:, :], ALU.mult))
                        em.op("vector", lambda e: e.tensor_tensor(ost[:, :], ost[:, :], xg[0][:, :], ALU.add), inc=v2)
                        em.wait("sync", v2)
                        em.dma("sync", out_d[128 * j:128 * (j + 1), :], ost[:, :], out_sems[0])
                        ots_free = stx.n
                        lrc_free = lrc.n
                    else:
                        em.wait("vector", lev)
                        em.op("vector", lambda e: e.tensor_single_scalar(lsb[0:1, :], lsb[0:1, :], 1e-6, ALU.add))
                        em.op("vector", lambda e: e.reciprocal(lsb[0:1, :], lsb[0:1, :]), inc=lrc)
                        em.wait("tensor", lrc)
                        em.wait("tensor", exd)
                        for g in range(4):
                            em.op("tensor", lambda e, g_=g: e.matmul(
                                ps_sc[g_][:, 0:512], ones_f[0:1, 0:128],
                                lsb[0:1, 512 * g_:512 * (g_ + 1)],
                                start=True, stop=True), inc=bcd)
                        em.wait("vector", bcd)
                        em.wait("vector", oev)
                        for g in range(4):
                            em.op("vector", lambda e, g_=g, j_=j: e.tensor_tensor(
                                sTn[:, 2048 * j_ + 512 * g_:2048 * j_ + 512 * (g_ + 1)],
                                oTs[:, 512 * g_:512 * (g_ + 1)], ps_sc[g_][:, 0:512], ALU.mult),
                                inc=(cmb if g == 3 else None))
                        ots_free = cmb.n
                    sc_free = cmb.n
                    oev_free = oev.n

            if layer == 1:
                ps_mm = [esB.enter_context(nc.psum_tensor(L + f"mm{i}", [128, 512], F32)) for i in range(2)]
                em.wait("tensor", cmb)
                grp = [0]
                ev_free = [0, 0]
                for i in range(8):
                    if i == 0:
                        em.wait("tensor", wol)
                    em.wait("sync", v2, v2.n)
                    em.wait("sync", st_g, marks["g_st1"])
                    em.dma("sync", xg[0][:, :], xsrc[128 * i:128 * (i + 1), :], ldg)
                    em.dma("sync", xg[1][:, :], gsrc[128 * i:128 * (i + 1), :], ldg)
                    ldxg = ldg.n
                    em.wait("scalar", v2, v2.n)
                    for nb in range(4):
                        p2 = grp[0] % 2
                        grp[0] += 1
                        em.wait("tensor", evq, ev_free[p2])
                        for ccn in range(16):
                            em.op("tensor", lambda e, c=ccn, n=nb, p=p2, i_=i: e.matmul(
                                ps_mm[p][:, 0:512],
                                sTn[:, 2048 * i_ + 128 * c:2048 * i_ + 128 * (c + 1)],
                                wo_sb[:, 2048 * c + 512 * n:2048 * c + 512 * (n + 1)],
                                start=(c == 0), stop=(c == 15)), inc=(pe if ccn == 15 else None))
                        em.wait("scalar", pe)
                        em.op("scalar", lambda e, p=p2, c=512 * nb: e.copy(
                            ssb[:, c:c + 512], ps_mm[p][:, 0:512]), inc=evq)
                        ev_free[p2] = evq.n
                    em.wait("vector", evq)
                    em.wait("vector", ldg, ldxg)
                    em.wait("vector", out_sems[0], 16 * i)
                    em.op("vector", lambda e: e.tensor_tensor(ost[:, :], ssb[:, :], xg[1][:, :], ALU.mult))
                    em.op("vector", lambda e: e.tensor_tensor(ost[:, :], ost[:, :], xg[0][:, :], ALU.add), inc=v2)
                    em.wait("sync", v2)
                    em.dma("sync", out_d[128 * i:128 * (i + 1), :], ost[:, :], out_sems[0])

            if layer == 0:
                marks["x1_st0"] = out_sems[0].n
                marks["x1_st1"] = out_sems[1].n
            else:
                marks["xo_sems"] = (out_sems[0], out_sems[0].n, out_sems[1], out_sems[1].n)
            esB.close()

        def stage_barrier():
            em.barrier([v, scs, sc, pe, evq, evt, st, st_w, st_s[0], st_s[1], st_g,
                        st_o[0], st_o[1], ldk[0], ldk[1], gld])

        stage_prep()
        stage_barrier()
        stage_A(0)
        stage_barrier()
        stage_B(0)
        stage_barrier()
        stage_A(1)
        stage_barrier()
        stage_B(1)

        so0, n0, so1, n1 = marks["xo_sems"]
        em.wait("gpsimd", so0, n0)
        em.wait("gpsimd", so1, n1)
        em.finalize()
    return nc



_PROGS = {}


def _get_prog(key, builder, *args):
    if key not in _PROGS:
        _PROGS[key] = builder(*args)
    return _PROGS[key]


def _build_indices(c, layer):
    kidx = np.zeros((128, NKT), np.int32)
    vidx = np.zeros((128, NKT), np.int32)
    offs = slot_offsets()
    rmap = {}
    for o in range(8):
        for jo, (so, wo) in enumerate(own_tiles(o)):
            rmap[(so, wo)] = (o, jo)
    p = np.arange(128, dtype=np.int32)
    for j, (s, w) in enumerate(own_tiles(c)):
        koff, L = offs[j]
        order = [w] + list(range(w))
        for t in range(L):
            a = koff + t
            if t < len(order):
                o, jown = rmap[(s, order[t])]
                kidx[:, a] = o * 2176 + jown * 128 + p
                vidx[:, a] = o * 2176 + 1024 + jown * 128 + p
            else:
                kidx[:, a] = p
                vidx[:, a] = 2048 + p
    return kidx, vidx


def kernel(L0_attn_x, L0_attn_Wq, L0_attn_bq, L0_attn_Wk, L0_attn_bk, L0_attn_Wv,
           L0_attn_bv, L0_ffn_weight, L0_ffn_bias, L0_ffn_x_tensor_scale,
           L1_attn_Wq, L1_attn_Wk, L1_attn_Wv, L1_attn_Wo, L1_ffn_kernel, L1_ffn_bias):
    x = np.ascontiguousarray(np.asarray(L0_attn_x, np.float32))
    ident = np.eye(128, dtype=BF)
    ident_f = np.eye(128, dtype=np.float32)
    maskT = np.triu(np.ones((128, 128), dtype=np.float32)).astype(BF)
    cores = list(range(8))

    w0 = np.ascontiguousarray(np.asarray(L0_ffn_weight, np.float32))
    k1 = np.ascontiguousarray(np.asarray(L1_ffn_kernel, np.float32))
    wq1 = np.ascontiguousarray(np.asarray(L1_attn_Wq, np.float32))
    wk1 = np.ascontiguousarray(np.asarray(L1_attn_Wk, np.float32))
    wv1 = np.ascontiguousarray(np.asarray(L1_attn_Wv, np.float32))
    wo1 = np.ascontiguousarray(np.asarray(L1_attn_Wo, np.float32))

    nc = _get_prog("fused", build_fused)
    im = []
    for c in cores:
        sl = slice(256 * c, 256 * (c + 1))
        xo = np.ascontiguousarray(np.concatenate(
            [x[s, 128 * w:128 * (w + 1)] for (s, w) in own_tiles(c)], axis=0))
        k0i, v0i = _build_indices(c, 0)
        k1i, v1i = _build_indices(c, 1)
        im.append({
            "x_own": xo,
            "wq": np.ascontiguousarray(np.asarray(L0_attn_Wq, np.float32).transpose(1, 0, 2)),
            "wk": np.ascontiguousarray(np.asarray(L0_attn_Wk, np.float32).transpose(1, 0, 2)),
            "wv": np.ascontiguousarray(np.asarray(L0_attn_Wv, np.float32).transpose(1, 0, 2)),
            "bq_row": np.asarray(L0_attn_bq, np.float32).reshape(1, E),
            "bk_row": np.asarray(L0_attn_bk, np.float32).reshape(1, 512),
            "bv_row": np.asarray(L0_attn_bv, np.float32).reshape(1, 512),
            "ffn_bias0": np.asarray(L0_ffn_bias, np.float32).reshape(1, E),
            "xts": np.asarray(L0_ffn_x_tensor_scale, np.float32).reshape(1, 1),
            "ffn_bias1": np.asarray(L1_ffn_bias, np.float32).reshape(1, E),
            "w0_full": w0, "w0_own": np.ascontiguousarray(w0[sl]),
            "k1_own": np.ascontiguousarray(k1[:, sl]),
            "wq1_own": np.ascontiguousarray(wq1[sl]),
            "wk1_own": np.ascontiguousarray(wk1[sl]),
            "wv1_own": np.ascontiguousarray(wv1[sl]),
            "wo1_own": np.ascontiguousarray(wo1[sl]),
            "ident": ident, "ident_f": ident_f, "maskT": maskT,
            "kidx0": k0i, "vidx0": v0i, "kidx1": k1i, "vidx1": v1i,
        })
    res = run_bass_kernel_spmd(nc, im, core_ids=cores).results

    out = np.empty((B, S, E), np.float32)
    for c in cores:
        for j, (s, w) in enumerate(own_tiles(c)):
            out[s, 128 * w:128 * (w + 1)] = res[c]["xo"][128 * j:128 * (j + 1)]
    return out


# version 18
# speedup vs baseline: 1.0436x; 1.0436x over previous
import sys
sys.path.insert(0, "/opt/trn_rl_repo")
from contextlib import ExitStack
import numpy as np
import ml_dtypes
import concourse.bass as bass
import concourse.mybir as mybir
from concourse.bass_utils import run_bass_kernel_spmd
from concourse.bass_core_rust import IndirectOffsetOnAxis

F32 = mybir.dt.float32
BF16 = mybir.dt.bfloat16
I32 = mybir.dt.int32
AF = mybir.ActivationFunctionType
ALU = mybir.AluOpType
AX = mybir.AxisListType
BF = ml_dtypes.bfloat16

B, S, E = 4, 2048, 2048
RSQ = 0.08838834764831845
LN2 = 0.6931471805599453
INVLN2 = 1.4426950408889634
SLOT_LENS = (16, 14, 12, 10, 8, 6, 4, 2)
NKT = sum(SLOT_LENS)
ALLC = [[0, 1, 2, 3, 4, 5, 6, 7]]


def own_tiles(c):
    out = []
    for j in range(8):
        Lj = SLOT_LENS[j]
        heavy = (j % 2 == 0) == (c < 4)
        out.append((c % 4, (Lj - 1) if heavy else (Lj - 2)))
    return out


def slot_offsets():
    out, off = [], 0
    for L in SLOT_LENS:
        out.append((off, L))
        off += L
    return out


class SemC:
    def __init__(self, h):
        self.h = h
        self.n = 0


class Em:
    ENGS = ("sync", "gpsimd", "vector", "scalar", "tensor")
    SELF_SYNC = ("vector", "scalar")

    def __init__(self, nc, es):
        self.nc = nc
        self.es = es
        self.st = {k: [] for k in self.ENGS}
        self._sems = {}
        self._last = {}

    def sem(self, name):
        if isinstance(name, SemC):
            return name
        if name not in self._sems:
            h = self.es.enter_context(self.nc.semaphore(name))
            self._sems[name] = SemC(h)
        return self._sems[name]

    def op(self, eng, build, inc=None, k=1):
        if eng in self.SELF_SYNC:
            prev = self._last.get(eng)
            if prev is not None:
                self.st[eng].append(lambda e, h=prev[0], vv=prev[1]: e.wait_ge(h, vv))
            if inc is None:
                inc = eng + "_selfsync"
        if inc is not None:
            s = self.sem(inc)
            s.n += k
            self.st[eng].append(lambda e, b=build, h=s.h, kk=k: b(e).then_inc(h, kk))
            if eng in self.SELF_SYNC:
                self._last[eng] = (s.h, s.n)
        else:
            self.st[eng].append(lambda e, b=build: b(e))

    def dma(self, eng, out, in_, inc, **kw):
        self.op(eng, lambda e, o=out, i=in_, kw2=dict(kw): e.dma_start(out=o, in_=i, **kw2),
                inc, 16)

    def wait(self, eng, sem, val=None):
        s = self.sem(sem)
        v = s.n if val is None else val
        if v > 0:
            self.st[eng].append(lambda e, h=s.h, vv=v: e.wait_ge(h, vv))

    def barrier(self, sems):
        for eng in self.ENGS:
            for s in sems:
                self.wait(eng, s)

    def finalize(self):
        with self.nc.Block() as block:
            @block.sync
            def _(e):
                for f in self.st["sync"]:
                    f(e)

            @block.gpsimd
            def _(e):
                for f in self.st["gpsimd"]:
                    f(e)

            @block.vector
            def _(e):
                for f in self.st["vector"]:
                    f(e)

            @block.scalar
            def _(e):
                for f in self.st["scalar"]:
                    f(e)

            @block.tensor
            def _(e):
                for f in self.st["tensor"]:
                    f(e)



def emit_round(em, eng, v_ap, r_out, i_scr, f1, f2, inc=None):
    em.op(eng, lambda e: e.tensor_scalar(i_scr, v_ap, 0.5, None, ALU.add))
    em.op(eng, lambda e: e.tensor_tensor(f1, v_ap, i_scr, ALU.subtract))
    em.op(eng, lambda e: e.tensor_single_scalar(f2, f1, 0.5, ALU.is_ge))
    em.op(eng, lambda e: e.tensor_single_scalar(f1, f1, -0.5, ALU.is_lt))
    em.op(eng, lambda e: e.tensor_tensor(f2, f2, f1, ALU.subtract))
    em.op(eng, lambda e: e.tensor_tensor(r_out, f2, i_scr, ALU.add), inc=inc)


def emit_e4m3(em, bamax_ap, inv6ts_col, invts_col, bsq, invd, scr, sv, ss, bm3, bp3):
    V, SC = "vector", "scalar"
    f1, f2, f3, i1 = scr["f1"][:, :], scr["f2"][:, :], scr["f3"][:, :], scr["i1"][:, :]
    bq = bsq[:, :]
    iv = invd[:, :]
    em.op(V, lambda e: e.tensor_scalar(f1, bamax_ap, inv6ts_col, None, ALU.mult))
    em.op(V, lambda e: e.tensor_scalar(f1, f1, 448.0, 2.44140625e-4, ALU.min, ALU.max), inc=sv)
    em.wait(SC, sv)
    em.op(SC, lambda e: e.activation(f2, f1, AF.Ln), inc=ss)
    em.wait(V, ss)
    em.op(V, lambda e: e.tensor_single_scalar(f2, f2, INVLN2, ALU.mult))
    em.op(V, lambda e: e.tensor_scalar(i1, f2, 0.0, None, ALU.add))
    em.op(V, lambda e: e.tensor_tensor(f3, f2, i1, ALU.subtract))
    em.op(V, lambda e: e.tensor_single_scalar(f3, f3, 0.0, ALU.is_lt))
    em.op(V, lambda e: e.tensor_tensor(f2, i1, f3, ALU.subtract))
    em.op(V, lambda e: e.tensor_scalar(f2, f2, -6.0, 8.0, ALU.max, ALU.min), inc=sv)
    em.wait(SC, sv)
    em.op(SC, lambda e: e.activation(f3, f2, AF.Exp, bias=bm3, scale=LN2))
    em.op(SC, lambda e: e.activation(f2, f2, AF.Exp, bias=bp3, scale=-LN2), inc=ss)
    em.wait(V, ss)
    em.op(V, lambda e: e.tensor_tensor(f1, f1, f2, ALU.mult))
    emit_round(em, V, f1, f1, i1, f2, bq)
    em.op(V, lambda e: e.tensor_tensor(bq, f1, f3, ALU.mult))
    em.op(V, lambda e: e.tensor_single_scalar(f2, bq, 0.0, ALU.is_equal))
    em.op(V, lambda e: e.tensor_tensor(f2, bq, f2, ALU.add))
    em.op(V, lambda e: e.reciprocal(f2, f2))
    em.op(V, lambda e: e.tensor_scalar(iv, f2, invts_col, None, ALU.mult))


def emit_e2m1(em, xin_ap, invd, bsq, xdq_ap, scr, sv, ss, inc=None):
    V, SC = "vector", "scalar"
    af, tf = scr["af"][:, :], scr["tf"][:, :]
    a2, c2, c4, f2 = (scr[k][:, :] for k in ("a2", "c2", "c4", "f2"))
    i1 = scr["i1"][:, :]

    def b3(t):
        return t[:, 0:128].rearrange("p (b one) -> p b one", one=1).broadcast_to([128, 128, 16])

    def v3(t):
        return t.rearrange("p (b k) -> p b k", k=16)

    em.op(V, lambda e: e.tensor_tensor(v3(af), v3(xin_ap), b3(invd), ALU.mult))
    em.op(V, lambda e: e.tensor_scalar(af, af, 6.0, -6.0, ALU.min, ALU.max), inc=sv)
    em.wait(SC, sv)
    em.op(SC, lambda e: e.activation(a2, af, AF.Square), inc=ss)
    em.wait(V, ss)
    em.op(V, lambda e: e.tensor_single_scalar(c2, a2, 4.0, ALU.is_lt))
    em.op(V, lambda e: e.tensor_single_scalar(c4, a2, 16.0, ALU.is_lt))
    em.op(V, lambda e: e.tensor_scalar(a2, c4, 0.5, 0.5, ALU.mult, ALU.add))
    em.op(V, lambda e: e.tensor_tensor(a2, a2, c2, ALU.add))
    em.op(V, lambda e: e.tensor_tensor(tf, af, a2, ALU.mult))
    em.op(V, lambda e: e.tensor_scalar(i1, tf, 0.5, None, ALU.add))
    em.op(V, lambda e: e.tensor_tensor(af, tf, i1, ALU.subtract))
    em.op(V, lambda e: e.tensor_single_scalar(f2, af, 0.5, ALU.is_ge))
    em.op(V, lambda e: e.tensor_single_scalar(af, af, -0.5, ALU.is_lt))
    em.op(V, lambda e: e.tensor_tensor(f2, f2, af, ALU.subtract))
    em.op(V, lambda e: e.tensor_tensor(tf, f2, i1, ALU.add))
    em.op(V, lambda e: e.tensor_scalar(c4, c4, -1.0, 2.0, ALU.mult, ALU.add))
    em.op(V, lambda e: e.scalar_tensor_tensor(c4, c2, -0.5, c4, ALU.mult, ALU.add))
    em.op(V, lambda e: e.tensor_tensor(tf, tf, c4, ALU.mult))
    em.op(V, lambda e: e.tensor_tensor(v3(xdq_ap), v3(tf), b3(bsq), ALU.mult), inc=inc)


def emit_int8_cols(em, amax_ap, inv_ap, scale_ap, s0, s1):
    V = "vector"
    em.op(V, lambda e: e.tensor_single_scalar(s0, amax_ap, 0.0, ALU.is_equal))
    em.op(V, lambda e: e.tensor_tensor(s1, amax_ap, s0, ALU.add))
    em.op(V, lambda e: e.reciprocal(s1, s1))
    em.op(V, lambda e: e.tensor_single_scalar(s1, s1, 127.0, ALU.mult))
    em.op(V, lambda e: e.scalar_tensor_tensor(inv_ap, s0, -126.0, s1, ALU.mult, ALU.add))
    em.op(V, lambda e: e.tensor_single_scalar(s1, amax_ap, 1.0 / 127.0, ALU.mult))
    em.op(V, lambda e: e.tensor_tensor(scale_ap, s1, s0, ALU.add))



def build_fused():
    nc = bass.Bass("TRN2", target_bir_lowering=False)
    D = {}

    def din(n, sh, dt=F32):
        D[n] = nc.dram_tensor(n, list(sh), dt, kind="ExternalInput").ap()

    def dout(n, sh, dt=F32):
        D[n] = nc.dram_tensor(n, list(sh), dt, kind="ExternalOutput").ap()

    def dint(n, sh, dt=F32):
        h = nc.dram_tensor(n, list(sh), dt)
        D[n] = h.ap()
        return h

    din("x_own", (1024, E))
    din("wq", (E, 16, 128))
    din("wk", (E, 4, 128))
    din("wv", (E, 4, 128))
    din("bq_row", (1, E))
    din("bk_row", (1, 512))
    din("bv_row", (1, 512))
    din("ffn_bias0", (1, E))
    din("xts", (1, 1))
    din("ffn_bias1", (1, E))
    din("w0_full", (E, E))
    din("w0_own", (256, E))
    din("k1_own", (E, 256))
    din("wq1_own", (256, E))
    din("wk1_own", (256, E))
    din("wv1_own", (256, E))
    din("wo1_own", (256, E))
    din("ident", (128, 128), BF16)
    din("ident_f", (128, 128))
    din("maskT", (128, 128), BF16)
    din("kidx0", (128, NKT), I32)
    din("vidx0", (128, NKT), I32)
    din("kidx1", (128, NKT), I32)
    din("vidx1", (128, NKT), I32)
    dout("xo", (1024, E))

    h_cprep = dint("cprep", (6 * E, 256), BF16)
    h_gprep = dint("gprep", (8 * 6 * E, 256), BF16)
    h_cks = dint("cks", (256, 1))
    h_gks = dint("gks", (2048, 1))
    dint("wtr", (1, 4))
    dint("qfm_d0", (E, 1024), BF16)
    dint("qfm_d1", (E, 1024), BF16)
    dint("g_d0", (1024, E))
    dint("g_d1", (1024, E))
    dint("x1_d", (1024, E))
    h_ckv = [dint("ckv0", (2176, 520), BF16), dint("ckv1", (2176, 2080), BF16)]
    h_gkv = [dint("gkv0", (8 * 2176, 520), BF16), dint("gkv1", (8 * 2176, 2080), BF16)]

    with ExitStack() as es:
        em = Em(nc, es)
        sb = lambda n, sh, dt=F32: es.enter_context(nc.sbuf_tensor(n, sh, dt))

        id_sb = sb("id_sb", [128, 128], BF16)
        id_f = sb("id_f", [128, 128])
        msk = sb("msk", [128, 128], BF16)
        ones_f = sb("ones_f", [1, 512])
        ones_bf = sb("ones_bf", [1, 512], BF16)
        cst = sb("cst", [128, 4])

        cc = em.sem("cc")
        ld_id = em.sem("ld_id")
        st = em.sem("st")
        st_w = em.sem("st_w")
        pe = em.sem("pe")
        v = em.sem("v")
        scs = em.sem("scs")
        sc = em.sem("sc")
        sv = em.sem("sv")
        ss = em.sem("ss")
        evq = em.sem("evq")
        evt = em.sem("evt")
        gld = em.sem("gld")
        st_s = [em.sem("st_s0"), em.sem("st_s1")]
        st_g = em.sem("st_g")
        st_o = [em.sem("st_o0"), em.sem("st_o1")]
        ldk = [em.sem("ldk0"), em.sem("ldk1")]

        em.dma("sync", id_sb[:, :], D["ident"][:, :], ld_id)
        em.dma("sync", id_f[:, :], D["ident_f"][:, :], ld_id)
        em.dma("sync", msk[:, :], D["maskT"][:, :], ld_id)
        em.op("vector", lambda e: e.memset(ones_f[:, :], 1.0))
        em.op("vector", lambda e: e.memset(ones_bf[:, :], 1.0))
        em.op("vector", lambda e: e.memset(cst[:, 0:1], -3.0 * LN2))
        em.op("vector", lambda e: e.memset(cst[:, 1:2], 3.0 * LN2))
        em.op("vector", lambda e: e.memset(cst[:, 2:3], 1e-5))
        em.op("vector", lambda e: e.memset(cst[:, 3:4], 0.1), inc=v)
        em.wait("scalar", v)
        em.wait("tensor", ld_id)
        em.wait("vector", ld_id)

        marks = {}

        def stage_prep():
            esP = ExitStack()
            sbP = lambda n, sh, dt=F32: esP.enter_context(nc.sbuf_tensor(n, sh, dt))
            psP = lambda n, sh, dt=F32: esP.enter_context(nc.psum_tensor(n, sh, dt))
            wf = [sbP(f"p_wf{i}", [128, E]) for i in range(2)]
            wbf = sbP("p_wbf", [128, E], BF16)
            bamax = sbP("p_bamax", [128, E])
            colmax = sbP("p_colmax", [128, 1])
            rowmax = sbP("p_rowmax", [1, 128])
            wts_sb = sbP("p_wts", [1, 4])
            inv6ts_col = sbP("p_i6", [128, 1])
            invts_col = sbP("p_it", [128, 1])
            e4scr = {"f1": sbP("p_e41", [128, 128]), "f2": sbP("p_e42", [128, 128]),
                     "f3": sbP("p_e43", [128, 128]), "i1": sbP("p_e4i", [128, 128], I32)}
            pf_a = sbP("p_pfa", [128, E])
            pf_t = sbP("p_pft", [128, E])
            qscr = {k: sbP("p_q" + k, [128, E], BF16) for k in ("a2", "c2", "c4", "f2")}
            qscr["af"] = pf_a
            qscr["tf"] = pf_t
            qscr["i1"] = sbP("p_qi1", [128, E], I32)
            bsq = sbP("p_bsq", [128, 128])
            invd = sbP("p_invd", [128, 128])
            wdq = sbP("p_wdq", [128, E], BF16)
            stage = sbP("p_stage", [128, 16 * 256], BF16)
            kT = [sbP(f"p_kT{i}", [128, E]) for i in range(2)]
            kqb = sbP("p_kqb", [128, E], BF16)
            kamax = sbP("p_kamax", [128, 2])
            kinv = sbP("p_kinv", [128, 2])
            kscale = sbP("p_kscale", [128, 2])
            small = sbP("p_small", [128, 8])
            ps_t = [psP(f"p_pst{i}", [128, 1024], BF16) for i in range(2)]
            ps_tf = [psP(f"p_pstf{i}", [128, 512]) for i in range(2)]
            ps_c = psP("p_psc", [128, 512])
            ld_w0 = em.sem("ld_w0")
            ld_w1 = em.sem("ld_w1")

            cprep = D["cprep"]

            def store_stage_rows(iw):
                em.wait("sync", sc)
                for kj in range(16):
                    em.dma("sync", cprep[2048 * iw + 128 * kj:2048 * iw + 128 * (kj + 1), :],
                           stage[:, 256 * kj:256 * (kj + 1)], st)

            vmarks = []
            for ch in range(16):
                par = ch % 2
                if ch >= 2:
                    em.wait("sync", v, vmarks[ch - 2])
                em.dma("sync", wf[par][:, :], D["w0_full"][128 * ch:128 * (ch + 1), :],
                       ld_w0 if par == 0 else ld_w1)
                em.wait("vector", ld_w0 if par == 0 else ld_w1)
                em.op("vector", lambda e, c=ch, p=par: e.tensor_reduce(
                    bamax[:, 128 * c:128 * (c + 1)].rearrange("p (b k) -> p b k", k=16),
                    wf[p][:, :].rearrange("p (b k) -> p b k", k=16),
                    AX.X, ALU.max, apply_absolute_value=True), inc=v)
                vmarks.append(v.n)
            em.op("vector", lambda e: e.tensor_reduce(colmax[:, 0:1], bamax[:, :], AX.X,
                                                      ALU.max, apply_absolute_value=True), inc=v)
            em.wait("tensor", v)
            em.op("tensor", lambda e: e.transpose(ps_tf[0][0:1, 0:128], colmax[:, 0:1], id_f[:, :]), inc=pe)
            em.wait("vector", pe)
            em.op("vector", lambda e: e.tensor_copy(rowmax[0:1, :], ps_tf[0][0:1, 0:128]))
            em.op("vector", lambda e: e.tensor_reduce(wts_sb[0:1, 0:1], rowmax[0:1, :], AX.X,
                                                      ALU.max, apply_absolute_value=True))
            em.op("vector", lambda e: e.tensor_single_scalar(wts_sb[0:1, 0:1], wts_sb[0:1, 0:1],
                                                             1.0 / 2688.0, ALU.mult))
            em.op("vector", lambda e: e.tensor_single_scalar(wts_sb[0:1, 1:2], wts_sb[0:1, 0:1],
                                                             6.0, ALU.mult))
            em.op("vector", lambda e: e.reciprocal(wts_sb[0:1, 1:2], wts_sb[0:1, 1:2]))
            em.op("vector", lambda e: e.reciprocal(wts_sb[0:1, 2:3], wts_sb[0:1, 0:1]), inc=v)
            em.wait("sync", v)
            em.dma("sync", D["wtr"][0:1, 0:1], wts_sb[0:1, 0:1], st_w)
            marks["wts_store"] = st_w.n
            em.wait("tensor", v)
            em.op("tensor", lambda e: e.matmul(ps_c[:, 0:1], ones_f[0:1, 0:128], wts_sb[0:1, 1:2],
                                               start=True, stop=True))
            em.op("tensor", lambda e: e.matmul(ps_c[:, 1:2], ones_f[0:1, 0:128], wts_sb[0:1, 2:3],
                                               start=True, stop=True), inc=pe)
            em.wait("vector", pe)
            em.op("vector", lambda e: e.tensor_copy(inv6ts_col[:, 0:1], ps_c[:, 0:1]))
            em.op("vector", lambda e: e.tensor_copy(invts_col[:, 0:1], ps_c[:, 1:2]), inc=v)

            for cci in range(2):
                em.wait("sync", v, v.n)
                em.dma("sync", wf[0][:, :], D["w0_own"][128 * cci:128 * (cci + 1), :], ld_w0)
                em.wait("vector", ld_w0)
                em.op("vector", lambda e: e.tensor_reduce(
                    bamax[:, 0:128].rearrange("p (b k) -> p b k", k=16),
                    wf[0][:, :].rearrange("p (b k) -> p b k", k=16),
                    AX.X, ALU.max, apply_absolute_value=True))
                emit_e4m3(em, bamax[:, 0:128], inv6ts_col[:, 0:1], invts_col[:, 0:1],
                          bsq, invd, e4scr, sv, ss, cst[:, 0:1], cst[:, 1:2])
                emit_e2m1(em, wf[0][:, :], invd, bsq, wdq[:, :], qscr, sv, ss, inc=v)
                em.wait("tensor", v)
                for q4 in range(4):
                    pst = ps_t[q4 % 2]
                    em.wait("tensor", sc, sc.n - 1)
                    for j4 in range(4):
                        j = 4 * q4 + j4
                        em.op("tensor", lambda e, jj=j, p=pst, o=j4: e.transpose(
                            p[:, 128 * o:128 * (o + 1)], wdq[:, 128 * jj:128 * (jj + 1)],
                            id_sb[:, :]), inc=(pe if j4 == 3 else None))
                    em.wait("scalar", pe)
                    em.op("scalar", lambda e, q=q4, c=cci, p=pst: e.copy(
                        stage[:, :].rearrange("p (kj n) -> p kj n", n=256)[:, 4 * q:4 * (q + 1), 128 * c:128 * (c + 1)],
                        p[:, 0:512].rearrange("p (j o) -> p j o", o=128)), inc=sc)
            store_stage_rows(0)

            em.wait("sync", v, v.n)
            em.wait("sync", pe, pe.n)
            for dc in range(16):
                em.wait("sync", pe, pe.n)
                em.dma("sync", wf[1][:, 0:256], D["k1_own"][128 * dc:128 * (dc + 1), :], ld_w1)
                em.wait("tensor", ld_w1)
                em.wait("tensor", sc, sc.n)
                for fb in range(2):
                    em.op("tensor", lambda e, f=fb: e.transpose(
                        ps_tf[f][:, 0:128], wf[1][:, 128 * f:128 * (f + 1)], id_f[:, :]),
                        inc=(pe if fb == 1 else None))
                em.wait("scalar", pe)
                for fb in range(2):
                    em.op("scalar", lambda e, f=fb, d=dc: e.copy(
                        kT[f][:, 128 * d:128 * (d + 1)], ps_tf[f][:, 0:128]), inc=sc)
            em.wait("vector", sc)
            for fb in range(2):
                em.op("vector", lambda e, f=fb: e.tensor_reduce(
                    kamax[:, f:f + 1], kT[f][:, :], AX.X, ALU.max, apply_absolute_value=True))
                emit_int8_cols(em, kamax[:, fb:fb + 1], kinv[:, fb:fb + 1], kscale[:, fb:fb + 1],
                               small[:, 0:1], small[:, 1:2])
                em.op("vector", lambda e, f=fb: e.tensor_scalar(
                    pf_a[:, :], kT[f][:, :], kinv[:, f:f + 1], None, ALU.mult))
                emit_round(em, "vector", pf_a[:, :], pf_t[:, :], qscr["i1"][:, :],
                           wf[0][:, :], qscr["f2"][:, :])
                em.op("vector", lambda e: e.tensor_scalar(pf_t[:, :], pf_t[:, :],
                                                          127.0, -127.0, ALU.min, ALU.max))
                em.op("vector", lambda e, f=fb: e.tensor_copy(kT[f][:, :], pf_t[:, :]), inc=v)
            em.wait("sync", v)
            em.dma("sync", D["cks"][0:128, :], kscale[:, 0:1], st)
            em.dma("sync", D["cks"][128:256, :], kscale[:, 1:2], st)
            em.wait("scalar", v)
            em.wait("scalar", st, st.n)
            em.wait("tensor", v)
            for dj in range(16):
                em.wait("scalar", pe, pe.n)
                for fb in range(2):
                    em.op("scalar", lambda e, d=dj, f=fb: e.copy(
                        kqb[:, 128 * f:128 * (f + 1)], kT[f][:, 128 * d:128 * (d + 1)]), inc=sc)
                em.wait("tensor", sc)
                for fb in range(2):
                    em.op("tensor", lambda e, f=fb: e.transpose(
                        ps_t[f][:, 0:128], kqb[:, 128 * f:128 * (f + 1)], id_sb[:, :]),
                        inc=(pe if fb == 1 else None))
                em.wait("scalar", pe)
                for fb in range(2):
                    em.op("scalar", lambda e, d=dj, f=fb: e.copy(
                        stage[:, 256 * d + 128 * f:256 * d + 128 * (f + 1)], ps_t[f][:, 0:128]), inc=sc)
            store_stage_rows(1)

            for wname, iw in (("wq1_own", 2), ("wk1_own", 3), ("wv1_own", 4), ("wo1_own", 5)):
                em.wait("scalar", st, st.n)
                for oc in range(2):
                    em.wait("gpsimd", pe, pe.n)
                    em.dma("gpsimd", wbf[:, :], D[wname][128 * oc:128 * (oc + 1), :], gld)
                    em.wait("tensor", gld)
                    for cj in range(16):
                        par = cj % 2
                        em.wait("tensor", sc, sc.n - 1)
                        em.op("tensor", lambda e, c=cj, p=par: e.transpose(
                            ps_t[p][:, 0:128], wbf[:, 128 * c:128 * (c + 1)], id_sb[:, :]), inc=pe)
                        em.wait("scalar", pe)
                        em.op("scalar", lambda e, c=cj, o=oc, p=par: e.copy(
                            stage[:, 256 * c + 128 * o:256 * c + 128 * (o + 1)], ps_t[p][:, 0:128]), inc=sc)
                store_stage_rows(iw)

            em.wait("gpsimd", st, st.n)
            em.op("gpsimd", lambda e: e.collective_compute(
                "AllGather", ALU.bypass, ALLC, [h_cprep.ap().opt()], [h_gprep.ap().opt()]), inc=cc)
            em.op("gpsimd", lambda e: e.collective_compute(
                "AllGather", ALU.bypass, ALLC, [h_cks.ap().opt()], [h_gks.ap().opt()]), inc=cc)
            marks["cc_prep"] = cc.n
            esP.close()

        def stage_A(layer):
            ngr = 4 if layer == 0 else 16
            vew = 130 * ngr
            esA = ExitStack()
            sbA = lambda n, sh, dt=F32: esA.enter_context(nc.sbuf_tensor(n, sh, dt))
            psA = lambda n, sh, dt=F32: esA.enter_context(nc.psum_tensor(n, sh, dt))
            L = f"a{layer}_"
            gsb = sbA(L + "gsb", [128, E])
            xnf = [sbA(L + f"xnf{i}", [128, E]) for i in range(2)]
            xq = [sbA(L + f"xq{i}", [128, E], BF16) for i in range(2)]
            xqT = [sbA(L + f"xqT{i}", [128, 16 * 128], BF16) for i in range(2)]
            stage = [sbA(L + f"stage{i}", [128, 1024], BF16) for i in range(2)]
            cols = sbA(L + "cols", [128, 8])
            cols2 = sbA(L + "cols2", [128, 4])
            stats = sbA(L + "stats", [128, 32])
            srow = sbA(L + "srow", [1, 8])
            ffnb_bf = sbA(L + "ffnb", [1, E], BF16)
            rows = sbA(L + "rows", [1, 4 * E])
            i1t = sbA(L + "i1t", [128, E], I32)
            f2b = sbA(L + "f2b", [128, E], BF16)
            vstage = sbA(L + "vstage", [128, 520], BF16)
            ztile = sbA(L + "ztile", [128, vew], BF16)
            if layer == 0:
                QA = sbA(L + "QA", [128, E])
                QT = sbA(L + "QT", [128, E])
                qscr = {"af": QA, "tf": QT, "i1": i1t, "f2": f2b,
                        "a2": sbA(L + "qa2", [128, E], BF16),
                        "c2": sbA(L + "qc2", [128, E], BF16),
                        "c4": sbA(L + "qc4", [128, E], BF16)}
                bamax = sbA(L + "bamax", [128, 128])
                bsq = sbA(L + "bsq", [128, 128])
                invd = sbA(L + "invd", [128, 128])
                e4scr = {"f1": sbA(L + "e41", [128, 128]), "f2": sbA(L + "e42", [128, 128]),
                         "f3": sbA(L + "e43", [128, 128]), "i1": sbA(L + "e4i", [128, 128], I32)}
                xtbuf = [gsb, QA]
            else:
                ysb = sbA(L + "ysb", [128, E])
                skb = sbA(L + "skb", [128, E])
                bqb = sbA(L + "bqb", [128, E])
                qf1 = sbA(L + "qf1", [128, E])
                xtbuf = [gsb, ysb]

            ps_mm = [psA(L + f"mm{i}", [128, 512]) for i in range(2)]
            ps_tr = [psA(L + f"tr{i}", [128, 1024], BF16) for i in range(2)]
            ps_bc = psA(L + "bc", [128, 512])

            ld_rows = em.sem(L + "ld_rows")
            ldx = [em.sem(L + "ldx0"), em.sem(L + "ldx1")]
            ld_r = [em.sem(L + "ld_r0"), em.sem(L + "ld_r1")]
            ld_rq = [em.sem(L + "ld_rq0"), em.sem(L + "ld_rq1")]
            ld_wf = em.sem(L + "ld_wf")
            ld_wf2 = em.sem(L + "ld_wf2")
            ld_xnf = [em.sem(L + "ld_xnf0"), em.sem(L + "ld_xnf1")]

            xsrc = D["x_own"] if layer == 0 else D["x1_d"]
            if layer == 1:
                em.wait("sync", st_o[0], marks["x1_st0"])
                em.wait("sync", st_o[1], marks["x1_st1"])
                em.wait("gpsimd", st_o[0], marks["x1_st0"])
                em.wait("gpsimd", st_o[1], marks["x1_st1"])

            if layer == 0:
                em.dma("sync", rows[0:1, 0:E], D["bq_row"][:, :], ld_rows)
                em.dma("sync", rows[0:1, E:E + 512], D["bk_row"][:, :], ld_rows)
                em.dma("sync", rows[0:1, E + 512:E + 1024], D["bv_row"][:, :], ld_rows)
                em.dma("sync", rows[0:1, 2 * E:3 * E], D["ffn_bias0"][:, :], ld_rows)
                em.wait("sync", st_w, marks["wts_store"])
                em.dma("sync", srow[0:1, 0:1], D["wtr"][0:1, 0:1], ld_rows)
                em.dma("sync", srow[0:1, 1:2], D["xts"][:, :], ld_rows)
            else:
                em.wait("sync", cc, marks["cc_prep"])
                em.dma("sync", rows[0:1, 0:E],
                       D["gks"].rearrange("(o f) z -> o (f z)", o=1), ld_rows)
                em.dma("sync", rows[0:1, E:2 * E], D["ffn_bias1"][:, :], ld_rows)
            rows_ld = ld_rows.n

            es2 = ExitStack()
            xnT = es2.enter_context(nc.sbuf_tensor(L + "xnT", [128, 16 * 1024], BF16))
            rows_bf = es2.enter_context(nc.sbuf_tensor(L + "rows_bf", [1, 2 * E], BF16))
            wR = [es2.enter_context(nc.sbuf_tensor(L + f"wR{i}", [128, 16 * 512], BF16)) for i in range(2)]

            xt_free = [(0, 0), (0, 0)]
            for i in range(8):
                par = i % 2
                em.wait("sync", v, xt_free[par][0])
                em.wait("sync", scs, xt_free[par][1])
                em.dma("sync", xtbuf[par][:, :], xsrc[128 * i:128 * (i + 1), :], ldx[par])
                em.wait("vector", ldx[par])
                em.op("vector", lambda e, p=par: e.tensor_reduce(cols[:, 0:1], xtbuf[p][:, :], AX.X, ALU.add))
                em.op("vector", lambda e: e.tensor_single_scalar(cols[:, 0:1], cols[:, 0:1], -1.0 / 2048.0, ALU.mult), inc=v)
                em.wait("scalar", ldx[par])
                em.wait("scalar", v)
                em.wait("scalar", pe, pe.n)
                em.op("scalar", lambda e, p=par: e.activation(xq[0][:, :], xtbuf[p][:, :], AF.Square,
                                                              accum_out=cols[:, 1:2]), inc=scs)
                em.wait("vector", scs)
                em.op("vector", lambda e: e.tensor_single_scalar(cols[:, 1:2], cols[:, 1:2], 1.0 / 2048.0, ALU.mult))
                em.op("vector", lambda e: e.tensor_tensor(cols[:, 2:3], cols[:, 0:1], cols[:, 0:1], ALU.mult))
                em.op("vector", lambda e: e.tensor_tensor(cols[:, 1:2], cols[:, 1:2], cols[:, 2:3], ALU.subtract), inc=v)
                em.wait("scalar", v)
                em.op("scalar", lambda e: e.activation(cols[:, 3:4], cols[:, 1:2], AF.Sqrt, bias=cst[:, 2:3]), inc=scs)
                em.wait("vector", scs)
                em.op("vector", lambda e, i_=i: e.reciprocal(stats[:, 2 * i_ + 1:2 * i_ + 2], cols[:, 3:4]))
                em.op("vector", lambda e, i_=i: e.tensor_copy(stats[:, 2 * i_:2 * i_ + 1], cols[:, 0:1]))
                em.op("vector", lambda e, i_=i, p=par: e.tensor_scalar(
                    xq[0][:, :], xtbuf[p][:, :],
                    stats[:, 2 * i_:2 * i_ + 1], stats[:, 2 * i_ + 1:2 * i_ + 2],
                    ALU.add, ALU.mult), inc=v)
                xt_free[par] = (v.n, scs.n)
                em.wait("tensor", v)
                for q4 in range(4):
                    pst = ps_tr[q4 % 2]
                    em.wait("tensor", evt, evt.n - 1)
                    for j4 in range(4):
                        j = 4 * q4 + j4
                        em.op("tensor", lambda e, jj=j, p=pst, o=j4: e.transpose(
                            p[:, 128 * o:128 * (o + 1)],
                            xq[0][:, 128 * jj:128 * (jj + 1)], id_sb[:, :]),
                            inc=(pe if j4 == 3 else None))
                    em.wait("scalar", pe)
                    em.op("scalar", lambda e, i_=i, q=q4, p=pst: e.copy(
                        xnT[:, :].rearrange("p (c t) -> p c t", t=1024)[:, 4 * q:4 * (q + 1), 128 * i_:128 * (i_ + 1)],
                        p[:, 0:512].rearrange("p (c o) -> p c o", o=128)), inc=evt)
            xnT_ready = evt.n
            ph1_pe = pe.n

            region_last_pe = [0, 0]

            def load_quarter_g(ridx, iw, qi, eng="sync"):
                sem = ld_r[ridx] if eng == "gpsimd" else ld_rq[ridx]
                em.wait(eng, pe, region_last_pe[ridx])
                em.wait(eng, cc, marks["cc_prep"])
                for ec in range(16):
                    for ri, rr in enumerate((2 * qi, 2 * qi + 1)):
                        em.dma(eng,
                               wR[ridx][:, 512 * ec + 256 * ri:512 * ec + 256 * (ri + 1)],
                               D["gprep"][rr * 6 * E + 2048 * iw + 128 * ec:rr * 6 * E + 2048 * iw + 128 * (ec + 1), :],
                               sem)
                return (sem, sem.n)

            def load_l0_wq_quarter(ridx, qi):
                em.wait("gpsimd", pe, region_last_pe[ridx])
                for ec in range(16):
                    em.dma("gpsimd",
                           wR[ridx][:, 512 * ec:512 * (ec + 1)].rearrange("p (h d) -> p h d", d=128),
                           D["wq"][128 * ec:128 * (ec + 1), 4 * qi:4 * (qi + 1), :], ld_r[ridx])
                return (ld_r[ridx], ld_r[ridx].n)

            def load_l0_kv(ridx, name):
                em.wait("gpsimd", pe, region_last_pe[ridx])
                for ec in range(16):
                    em.dma("gpsimd",
                           wR[ridx][:, 512 * ec:512 * (ec + 1)].rearrange("p (g d) -> p g d", d=128),
                           D[name][128 * ec:128 * (ec + 1), :, :], ld_r[ridx])
                return (ld_r[ridx], ld_r[ridx].n)

            def wait_mark(eng, mark):
                em.wait(eng, em.sem(mark[0]), mark[1])

            if layer == 0:
                em.wait("scalar", ld_rows, rows_ld)
                em.op("scalar", lambda e: e.copy(rows_bf[0:1, 0:E + 1024], rows[0:1, 0:E + 1024]), inc=scs)
            rows_bf_ready = scs.n
            em.op("vector", lambda e: e.memset(vstage[:, :], 0.0))
            em.op("vector", lambda e: e.memset(
                vstage[:, :].rearrange("t (h o) -> t h o", o=130)[:, :, 128:129], 1.0), inc=v)
            vst_ready = v.n

            em.wait("vector", ld_rows, rows_ld)
            if layer == 0:
                em.op("vector", lambda e: e.tensor_tensor(srow[0:1, 2:3], srow[0:1, 0:1], srow[0:1, 1:2], ALU.mult))
                em.op("vector", lambda e: e.tensor_copy(srow[0:1, 3:4], srow[0:1, 2:3]))
                em.op("vector", lambda e: e.reciprocal(srow[0:1, 3:4], srow[0:1, 3:4]))
                em.op("vector", lambda e: e.tensor_single_scalar(srow[0:1, 4:5], srow[0:1, 1:2], 6.0, ALU.mult))
                em.op("vector", lambda e: e.reciprocal(srow[0:1, 4:5], srow[0:1, 4:5]))
                em.op("vector", lambda e: e.tensor_copy(srow[0:1, 5:6], srow[0:1, 1:2]))
                em.op("vector", lambda e: e.reciprocal(srow[0:1, 5:6], srow[0:1, 5:6]))
                em.op("vector", lambda e: e.tensor_scalar(rows[0:1, 3 * E:4 * E], rows[0:1, 2 * E:3 * E],
                                                          srow[0:1, 3:4], None, ALU.mult), inc=v)
                em.wait("scalar", v)
                em.op("scalar", lambda e: e.copy(ffnb_bf[0:1, :], rows[0:1, 3 * E:4 * E]), inc=scs)
                ffnb_ready = scs.n
                em.wait("tensor", v)
                em.op("tensor", lambda e: e.matmul(ps_bc[:, 0:1], ones_f[0:1, 0:128], srow[0:1, 2:3], start=True, stop=True))
                em.op("tensor", lambda e: e.matmul(ps_bc[:, 1:2], ones_f[0:1, 0:128], srow[0:1, 4:5], start=True, stop=True))
                em.op("tensor", lambda e: e.matmul(ps_bc[:, 2:3], ones_f[0:1, 0:128], srow[0:1, 5:6], start=True, stop=True), inc=pe)
                em.wait("vector", pe)
                em.op("vector", lambda e: e.tensor_copy(cols2[:, 0:3], ps_bc[:, 0:3]), inc=v)
                bc_ready = v.n
            else:
                em.wait("tensor", ld_rows, rows_ld)
                for j in range(4):
                    if j > 0:
                        em.wait("tensor", v)
                    em.op("tensor", lambda e, j_=j: e.matmul(
                        ps_bc[:, 0:512], ones_f[0:1, 0:128], rows[0:1, 512 * j_:512 * (j_ + 1)],
                        start=True, stop=True), inc=pe)
                    em.wait("vector", pe)
                    em.op("vector", lambda e, j_=j: e.tensor_copy(skb[:, 512 * j_:512 * (j_ + 1)], ps_bc[:, 0:512]), inc=v)
                em.op("vector", lambda e: e.tensor_reduce(srow[0:1, 0:1], rows[0:1, E:2 * E], AX.X,
                                                          ALU.max, apply_absolute_value=True))
                emit_int8_cols(em, srow[0:1, 0:1], srow[0:1, 1:2], srow[0:1, 2:3],
                               srow[0:1, 3:4], srow[0:1, 4:5])
                em.op("vector", lambda e: e.tensor_scalar(rows[0:1, 2 * E:3 * E], rows[0:1, E:2 * E],
                                                          srow[0:1, 1:2], None, ALU.mult))
                emit_round(em, "vector", rows[0:1, 2 * E:3 * E], rows[0:1, 2 * E:3 * E],
                           i1t[0:1, 0:E], gsb[0:1, 0:E], ysb[0:1, 0:E])
                em.op("vector", lambda e: e.tensor_scalar(rows[0:1, 2 * E:3 * E], rows[0:1, 2 * E:3 * E],
                                                          127.0, -127.0, ALU.min, ALU.max))
                em.op("vector", lambda e: e.tensor_scalar(rows[0:1, 2 * E:3 * E], rows[0:1, 2 * E:3 * E],
                                                          srow[0:1, 2:3], None, ALU.mult), inc=v)
                for j in range(4):
                    em.wait("tensor", v)
                    em.op("tensor", lambda e, j_=j: e.matmul(
                        ps_bc[:, 0:512], ones_f[0:1, 0:128], rows[0:1, 2 * E + 512 * j_:2 * E + 512 * (j_ + 1)],
                        start=True, stop=True), inc=pe)
                    em.wait("vector", pe)
                    em.op("vector", lambda e, j_=j: e.tensor_copy(bqb[:, 512 * j_:512 * (j_ + 1)], ps_bc[:, 0:512]), inc=v)
                bc_ready = v.n
                ffnb_ready = scs.n


            ev_free = [0, 0]
            st_free = [0, 0]
            grp = [0]
            spar_ctr = [0]

            def mm_group(lhs_fn, rhs_fn, n_k, bias_lhs_rhs, evac_fn):
                p2 = grp[0] % 2
                grp[0] += 1
                em.wait("tensor", evq, ev_free[p2])
                for kk in range(n_k):
                    last = (kk == n_k - 1) and bias_lhs_rhs is None
                    em.op("tensor", lambda e, k=kk, p=p2, l=last: e.matmul(
                        ps_mm[p][:, 0:512], lhs_fn(k), rhs_fn(k),
                        start=(k == 0), stop=l), inc=(pe if last else None))
                if bias_lhs_rhs is not None:
                    bl, br = bias_lhs_rhs
                    em.op("tensor", lambda e, p=p2: e.matmul(
                        ps_mm[p][:, 0:512], bl, br, start=False, stop=True), inc=pe)
                em.wait("scalar", pe)
                evac_fn(p2)
                ev_free[p2] = evq.n

            ckv = D["ckv%d" % layer]
            kw_ = 128 * ngr

            def ck_dst(gc):
                return ckv[0:1024, 0:kw_].rearrange("(j p) (g k) -> p j g k", p=128, k=128)[:, :, gc, :]

            def proj_fm(ridx, mark, kind, oc0, noc, bias_off):
                wait_mark("tensor", mark)
                em.wait("tensor", evt, xnT_ready)
                if bias_off is not None:
                    em.wait("tensor", scs, rows_bf_ready)
                for oc in range(noc):
                    abs_oc = oc0 + oc
                    spar = spar_ctr[0] % 2
                    spar_ctr[0] += 1
                    em.wait("scalar", st_s[spar], st_free[spar])
                    for tb in range(2):
                        bias = None
                        if bias_off is not None:
                            bias = (rows_bf[0:1, bias_off + 128 * abs_oc:bias_off + 128 * (abs_oc + 1)],
                                    ones_bf[0:1, 0:512])
                        mm_group(
                            lambda k, oc_=oc, r=ridx: wR[r][:, 512 * k + 128 * oc_:512 * k + 128 * (oc_ + 1)],
                            lambda k, t=tb: xnT[:, 1024 * k + 512 * t:1024 * k + 512 * (t + 1)],
                            16, bias,
                            lambda p2, t=tb, sp=spar: em.op("scalar", lambda e, p=p2, t_=t, s_=sp: e.copy(
                                stage[s_][:, 512 * t_:512 * (t_ + 1)], ps_mm[p][:, 0:512]), inc=evq))
                    em.wait("sync", evq)
                    if kind == "q":
                        em.dma("sync", D["qfm_d%d" % layer][128 * abs_oc:128 * (abs_oc + 1), :],
                               stage[spar][:, :], st_s[spar])
                    else:
                        em.dma("sync", ck_dst(abs_oc),
                               stage[spar][:, :].rearrange("p (j k) -> p j k", k=128), st_s[spar])
                    st_free[spar] = st_s[spar].n
                region_last_pe[ridx] = pe.n

            st_v = em.sem(L + "st_v")

            def proj_tm(ridx, mark, ob, bias_off):
                wait_mark("tensor", mark)
                em.wait("tensor", evt, xnT_ready)
                if bias_off is not None:
                    em.wait("tensor", scs, rows_bf_ready)
                em.wait("scalar", v, vst_ready)
                for i in range(8):
                    em.wait("scalar", st_v, st_v.n)
                    bias = None
                    if bias_off is not None:
                        bias = (ones_bf[0:1, 0:128], rows_bf[0:1, bias_off:bias_off + 512])
                    mm_group(
                        lambda k, i_=i: xnT[:, 1024 * k + 128 * i_:1024 * k + 128 * (i_ + 1)],
                        lambda k, r=ridx: wR[r][:, 512 * k:512 * (k + 1)],
                        16, bias,
                        lambda p2: em.op("scalar", lambda e, p=p2: e.copy(
                            vstage[:, :].rearrange("t (h o) -> t h o", o=130)[:, :, 0:128],
                            ps_mm[p][:, 0:512].rearrange("t (h o) -> t h o", o=128)), inc=evq))
                    em.wait("sync", evq)
                    em.dma("sync",
                           ckv[1024 + 128 * i:1024 + 128 * (i + 1), 520 * ob:520 * (ob + 1)],
                           vstage[:, :], st_v)
                region_last_pe[ridx] = pe.n

            def emit_kv_collective():
                em.op("vector", lambda e: e.memset(ztile[:, :], 0.0), inc=v)
                em.wait("sync", v)
                em.dma("sync", ckv[2048:2176, 0:vew], ztile[:, :], st_g)
                jw = vew - kw_
                em.dma("sync", ckv[0:1024, kw_:vew].rearrange("(j p) c -> p j c", p=128),
                       ztile[:, 0:8 * jw].rearrange("p (j c) -> p j c", c=jw), st_g)
                em.wait("gpsimd", st_s[0], st_s[0].n)
                em.wait("gpsimd", st_s[1], st_s[1].n)
                em.wait("gpsimd", st_v, st_v.n)
                em.wait("gpsimd", st_g, st_g.n)
                em.op("gpsimd", lambda e, ly=layer: e.collective_compute(
                    "AllGather", ALU.bypass, ALLC, [h_ckv[ly].ap().opt()], [h_gkv[ly].ap().opt()]), inc=cc)
                marks["cc_kv%d" % layer] = cc.n

            if layer == 0:
                mk = load_l0_kv(0, "wk")
                mv = load_l0_kv(1, "wv")
                proj_fm(0, mk, "k", 0, 4, E)
                m0 = load_l0_wq_quarter(0, 0)
                proj_tm(1, mv, 0, E + 512)
                m1 = load_l0_wq_quarter(1, 1)
                emit_kv_collective()
                proj_fm(0, m0, "q", 0, 4, 0)
                m2 = load_l0_wq_quarter(0, 2)
                proj_fm(1, m1, "q", 4, 4, 0)
                m3 = load_l0_wq_quarter(1, 3)
                proj_fm(0, m2, "q", 8, 4, 0)
                proj_fm(1, m3, "q", 12, 4, 0)
            else:
                mk0 = load_quarter_g(0, 3, 0, "gpsimd")
                mk1 = load_quarter_g(1, 3, 1, "gpsimd")
                proj_fm(0, mk0, "k", 0, 4, None)
                mk2 = load_quarter_g(0, 3, 2, "gpsimd")
                proj_fm(1, mk1, "k", 4, 4, None)
                mk3 = load_quarter_g(1, 3, 3, "gpsimd")
                proj_fm(0, mk2, "k", 8, 4, None)
                mv0 = load_quarter_g(0, 4, 0, "gpsimd")
                proj_fm(1, mk3, "k", 12, 4, None)
                mv1 = load_quarter_g(1, 4, 1, "gpsimd")
                proj_tm(0, mv0, 0, None)
                mv2 = load_quarter_g(0, 4, 2, "gpsimd")
                proj_tm(1, mv1, 1, None)
                mv3 = load_quarter_g(1, 4, 3, "gpsimd")
                proj_tm(0, mv2, 2, None)
                mq0 = load_quarter_g(0, 2, 0, "gpsimd")
                proj_tm(1, mv3, 3, None)
                mq1 = load_quarter_g(1, 2, 1, "gpsimd")
                emit_kv_collective()
                proj_fm(0, mq0, "q", 0, 4, None)
                mq2 = load_quarter_g(0, 2, 2, "gpsimd")
                proj_fm(1, mq1, "q", 4, 4, None)
                mq3 = load_quarter_g(1, 2, 3, "gpsimd")
                proj_fm(0, mq2, "q", 8, 4, None)
                proj_fm(1, mq3, "q", 12, 4, None)

            qkv_pe, qkv_v, qkv_scs = pe.n, v.n, scs.n
            es2.close()
            wF = esA.enter_context(nc.sbuf_tensor(L + "wF", [128, 16 * 2048], BF16))
            iw_f = 0 if layer == 0 else 1
            def emit_wf_loads():
                for eng in ("sync", "gpsimd"):
                    em.wait(eng, pe, qkv_pe)
                    em.wait(eng, v, qkv_v)
                    em.wait(eng, scs, qkv_scs)
                    em.wait(eng, cc, marks["cc_prep"])
                for ec in range(16):
                    for rr in range(8):
                        eng, sem = ("sync", ld_wf) if rr < 4 else ("gpsimd", ld_wf2)
                        em.dma(eng, wF[:, 2048 * ec + 256 * rr:2048 * ec + 256 * (rr + 1)],
                               D["gprep"][rr * 6 * E + 2048 * iw_f + 128 * ec:rr * 6 * E + 2048 * iw_f + 128 * (ec + 1), :],
                               sem)
                return (ld_wf.n, ld_wf2.n)


            xnf_free = [0, 0]
            xq_pe_free = [ph1_pe, ph1_pe]
            qmark = {}
            post_mark = [0]

            def ffn_quant(i):
                par = i % 2
                em.wait("scalar", v, xnf_free[par])
                em.dma("scalar", xnf[par][:, :], xsrc[128 * i:128 * (i + 1), :], ld_xnf[par])
                em.wait("vector", ld_xnf[par])
                em.wait("vector", v, bc_ready)
                em.wait("vector", pe, xq_pe_free[par])
                em.op("vector", lambda e, i_=i, p=par: e.tensor_scalar(
                    xnf[p][:, :], xnf[p][:, :],
                    stats[:, 2 * i_:2 * i_ + 1], stats[:, 2 * i_ + 1:2 * i_ + 2],
                    ALU.add, ALU.mult))
                if layer == 0:
                    em.op("vector", lambda e, p=par: e.tensor_reduce(
                        bamax[:, :].rearrange("p (b k) -> p b k", k=16),
                        xnf[p][:, :].rearrange("p (b k) -> p b k", k=16),
                        AX.X, ALU.max, apply_absolute_value=True))
                    emit_e4m3(em, bamax[:, :], cols2[:, 1:2], cols2[:, 2:3], bsq, invd, e4scr, sv, ss,
                              cst[:, 0:1], cst[:, 1:2])
                    emit_e2m1(em, xnf[par][:, :], invd, bsq, xq[par][:, :], qscr, sv, ss, inc=v)
                else:
                    em.op("vector", lambda e, p=par: e.tensor_reduce(
                        cols[:, 0:1], xnf[p][:, :], AX.X, ALU.max, apply_absolute_value=True))
                    emit_int8_cols(em, cols[:, 0:1], cols[:, 1:2], stats[:, 16 + i:17 + i],
                                   cols[:, 2:3], cols[:, 3:4])
                    em.op("vector", lambda e, p=par: e.tensor_scalar(
                        xnf[p][:, :], xnf[p][:, :], cols[:, 1:2], None, ALU.mult))
                    emit_round(em, "vector", xnf[par][:, :], xq[par][:, :], i1t[:, :],
                               qf1[:, :], f2b[:, :])
                    em.op("vector", lambda e, p=par: e.tensor_scalar(
                        xq[p][:, :], xq[p][:, :], 127.0, -127.0, ALU.min, ALU.max), inc=v)
                qmark[i] = v.n
                xnf_free[par] = v.n

            def ffn_mm(i):
                par = i % 2
                em.wait("tensor", v, qmark[i])
                for q4 in range(4):
                    pst = ps_tr[q4 % 2]
                    em.wait("tensor", evt, evt.n - 1)
                    for j4 in range(4):
                        jq = 4 * q4 + j4
                        em.op("tensor", lambda e, jj=jq, p=pst, o=j4, pr=par: e.transpose(
                            p[:, 128 * o:128 * (o + 1)], xq[pr][:, 128 * jj:128 * (jj + 1)], id_sb[:, :]),
                            inc=(pe if j4 == 3 else None))
                    em.wait("scalar", pe)
                    em.op("scalar", lambda e, q=q4, p=pst, pr=par: e.copy(
                        xqT[pr][:, 512 * q:512 * (q + 1)], p[:, 0:512]), inc=evt)
                em.wait("tensor", evt)
                xq_pe_free[par] = pe.n
                em.wait("tensor", ld_wf, wf_mark[0])
                em.wait("tensor", ld_wf2, wf_mark[1])
                if layer == 0:
                    em.wait("tensor", scs, ffnb_ready)
                else:
                    em.wait("scalar", v, post_mark[0])
                for nb in range(4):
                    bias = None
                    if layer == 0:
                        bias = (ones_bf[0:1, 0:128], ffnb_bf[0:1, 512 * nb:512 * (nb + 1)])
                    if layer == 0:
                        evac = (lambda p2, c=512 * nb: em.op("scalar", lambda e, p=p2, c_=c: e.activation(
                            gsb[:, c_:c_ + 512], ps_mm[p][:, 0:512], AF.Silu,
                            scale=cols2[:, 0:1]), inc=evq))
                    else:
                        evac = (lambda p2, c=512 * nb, i_=i: em.op("scalar", lambda e, p=p2, c_=c, ii=i_: e.activation(
                            ysb[:, c_:c_ + 512], ps_mm[p][:, 0:512], AF.Copy,
                            scale=stats[:, 16 + ii:17 + ii]), inc=evq))
                    mm_group(
                        lambda k, pr=par: xqT[pr][:, 128 * k:128 * (k + 1)],
                        lambda k, n=nb: wF[:, 2048 * k + 512 * n:2048 * k + 512 * (n + 1)],
                        16, bias, evac)
                if layer == 0:
                    em.wait("sync", evq)
                    em.dma("sync", D["g_d0"][128 * i:128 * (i + 1), :], gsb[:, :], st_g)
                    em.wait("scalar", st_g)
                else:
                    em.wait("vector", evq)
                    em.wait("vector", st_g, st_g.n)
                    em.op("vector", lambda e: e.tensor_tensor(ysb[:, :], ysb[:, :], skb[:, :], ALU.mult))
                    em.op("vector", lambda e: e.tensor_tensor(ysb[:, :], ysb[:, :], bqb[:, :], ALU.add))
                    em.op("vector", lambda e: e.tensor_single_scalar(ysb[:, :], ysb[:, :], 0.0, ALU.max))
                    em.op("vector", lambda e: e.tensor_reduce(cols[:, 0:1], ysb[:, :], AX.X,
                                                              ALU.max, apply_absolute_value=True))
                    emit_int8_cols(em, cols[:, 0:1], cols[:, 1:2], cols[:, 4:5],
                                   cols[:, 2:3], cols[:, 3:4])
                    em.op("vector", lambda e: e.tensor_scalar(gsb[:, :], ysb[:, :], cols[:, 1:2], None, ALU.mult))
                    emit_round(em, "vector", gsb[:, :], xq[par][:, :], i1t[:, :],
                               ysb[:, :], f2b[:, :])
                    em.op("vector", lambda e, p=par: e.tensor_scalar(
                        xq[p][:, :], xq[p][:, :], 127.0, -127.0, ALU.min, ALU.max))
                    em.op("vector", lambda e, p=par: e.tensor_scalar(
                        ysb[:, :], xq[p][:, :], cols[:, 4:5], None, ALU.mult), inc=v)
                    post_mark[0] = v.n
                    em.wait("scalar", v)
                    em.op("scalar", lambda e: e.activation(gsb[:, :], ysb[:, :], AF.Silu), inc=scs)
                    em.wait("sync", scs)
                    em.dma("sync", D["g_d1"][128 * i:128 * (i + 1), :], gsb[:, :], st_g)
                    em.wait("scalar", st_g)

            PIPE_FFN = True
            if PIPE_FFN:
                ffn_quant(0)
                ffn_quant(1)
                wf_mark = emit_wf_loads()
                ffn_mm(0)
                for i in range(2, 8):
                    ffn_quant(i)
                    ffn_mm(i - 1)
                ffn_mm(7)
            else:
                wf_mark = emit_wf_loads()
                for i in range(8):
                    ffn_quant(i)
                    ffn_mm(i)
            marks["g_st%d" % layer] = st_g.n
            esA.close()

        def stage_B(layer):
            ngr = 4 if layer == 0 else 16
            vew = 130 * ngr
            esB = ExitStack()
            sbB = lambda n, sh, dt=F32: esB.enter_context(nc.sbuf_tensor(n, sh, dt))
            L = f"b{layer}_"
            qsb = sbB(L + "qsb", [128, 16 * 1024], BF16)
            ksb = [sbB(L + f"ksb{i}", [128, vew], BF16) for i in range(2)]
            vsb = [sbB(L + f"vsb{i}", [128, vew], BF16) for i in range(2)]
            ptb = [sbB(L + f"ptb{i}", [128, 2048], BF16) for i in range(2)]
            lsb = sbB(L + "lsb", [1, 2048])
            oTs = sbB(L + "oTs", [128, 2048], F32 if layer == 0 else BF16)
            ssb = sbB(L + "ssb", [128, E])
            xg = [sbB(L + f"xg{i}", [128, E]) for i in range(2)]
            ost = sbB(L + "ost", [128, E])
            kidx_sb = sbB(L + "kidx", [128, NKT], I32)
            vidx_sb = sbB(L + "vidx", [128, NKT], I32)
            if layer == 0:
                lrq = sbB(L + "lrq", [128, 16])
            else:
                sTn = sbB(L + "sTn", [128, 8 * 2048], BF16)
                wo_sb = sbB(L + "wo", [128, 16 * 2048], BF16)

            ld2 = em.sem(L + "ld2")
            ldg = em.sem(L + "ldg")
            wol = em.sem(L + "wol")
            ldix = em.sem(L + "ldix")
            pes = em.sem(L + "pes")
            exd = em.sem(L + "exd")
            mkd = em.sem(L + "mkd")
            lmm = em.sem(L + "lmm")
            lev = em.sem(L + "lev")
            ltr = em.sem(L + "ltr")
            stx = em.sem(L + "stx")
            oev = em.sem(L + "oev")
            lrc = em.sem(L + "lrc")
            bcd = em.sem(L + "bcd")
            cmb = em.sem(L + "cmb")
            v2 = em.sem(L + "v2")

            out_d = D["x1_d"] if layer == 0 else D["xo"]
            out_sems = st_o if layer == 0 else [em.sem("b1_sto0"), em.sem("b1_sto1")]
            xsrc = D["x_own"] if layer == 0 else D["x1_d"]
            gsrc = D["g_d%d" % layer]
            gkv = D["gkv%d" % layer]

            em.dma("sync", kidx_sb[:, :], D["kidx%d" % layer][:, :], ldix)
            em.dma("sync", vidx_sb[:, :], D["vidx%d" % layer][:, :], ldix)
            em.wait("sync", st_s[0], st_s[0].n)
            em.wait("sync", st_s[1], st_s[1].n)
            for h in range(16):
                em.dma("sync",
                       qsb[:, :].rearrange("p (j hq) -> p j hq", hq=2048)[:, :, 128 * h:128 * (h + 1)],
                       D["qfm_d%d" % layer][128 * h:128 * (h + 1), :].rearrange("p (j q) -> p j q", q=128),
                       ld2)
            if layer == 1:
                em.wait("sync", cc, marks["cc_prep"])
                for ec in range(16):
                    for rr in range(8):
                        em.dma("sync", wo_sb[:, 2048 * ec + 256 * rr:2048 * ec + 256 * (rr + 1)],
                               D["gprep"][rr * 6 * E + 2048 * 5 + 128 * ec:rr * 6 * E + 2048 * 5 + 128 * (ec + 1), :],
                               wol)
            q_ready = ld2.n
            em.wait("gpsimd", ldix)
            em.wait("gpsimd", cc, marks["cc_kv%d" % layer])

            with ExitStack() as ps_es:
                ps_sc = [ps_es.enter_context(nc.psum_tensor(L + f"sc{g}", [128, 512], F32))
                         for g in range(4)]
                ps_o = [ps_es.enter_context(nc.psum_tensor(L + f"po{g}", [128, 512], F32))
                        for g in range(4)]

                kv_free = [0, 0]
                av_hist = [0, 0]
                sc_free = 0
                oev_free = 0
                lrc_free = 0
                ots_free = 0
                for j, (koff, Lk) in enumerate(slot_offsets()):
                    qbase = 2048 * j
                    exd0 = exd.n
                    lev0 = lev.n
                    for kt in range(Lk):
                        par = kt % 2
                        kta = koff + kt
                        em.wait("gpsimd", lmm, kv_free[par])
                        em.op("gpsimd", lambda e, p=par, a=kta: e.indirect_dma_start(
                            out=ksb[p][:, :], out_offset=None, in_=gkv[:, :],
                            in_offset=IndirectOffsetOnAxis(kidx_sb[:, a:a + 1], 0)),
                            inc=ldk[par], k=16)
                        em.op("gpsimd", lambda e, p=par, a=kta: e.indirect_dma_start(
                            out=vsb[p][:, :], out_offset=None, in_=gkv[:, :],
                            in_offset=IndirectOffsetOnAxis(vidx_sb[:, a:a + 1], 0)),
                            inc=ldk[par], k=16)
                        pb = ptb[par]
                        em.wait("tensor", ldk[par])
                        if j == 0 and kt == 0:
                            em.wait("tensor", ld2, q_ready)
                        if kt == 0:
                            em.wait("tensor", cmb, sc_free)
                        else:
                            em.wait("tensor", exd, exd0 + 4 * kt)
                        pes0 = pes.n
                        if layer == 0:
                            for g in range(4):
                                if kt > 0:
                                    em.wait("tensor", lev, lev0 + 4 * (kt - 1) + g + 1)
                                em.op("tensor", lambda e, g_=g, p=par, qb=qbase: e.matmul(
                                    ps_sc[g_][:, 0:512],
                                    ksb[p][:, 128 * g_:128 * (g_ + 1)],
                                    qsb[:, qb + 512 * g_:qb + 512 * (g_ + 1)],
                                    start=True, stop=True), inc=pes)
                        else:
                            for h in range(16):
                                if kt > 0 and h % 4 == 0:
                                    em.wait("tensor", lev, lev0 + 4 * (kt - 1) + h // 4 + 1)
                                em.op("tensor", lambda e, h_=h, p=par, qb=qbase: e.matmul(
                                    ps_sc[h_ // 4][:, 128 * (h_ % 4):128 * (h_ % 4) + 128],
                                    ksb[p][:, 128 * h_:128 * (h_ + 1)],
                                    qsb[:, qb + 128 * h_:qb + 128 * (h_ + 1)],
                                    start=(h_ % 4 == 0), stop=(h_ % 4 == 3)), inc=pes)
                        gstep = 1 if layer == 0 else 4
                        em.wait("scalar", lmm, av_hist[0])
                        exd_k = []
                        mkd_k = []
                        for g in range(4):
                            em.wait("scalar", pes, pes0 + gstep * (g + 1))
                            if layer == 0:
                                em.op("scalar", lambda e, g_=g, b=pb: e.activation(
                                    b[:, 512 * g_:512 * (g_ + 1)], ps_sc[g_][:, 0:512],
                                    AF.Exp, scale=RSQ), inc=exd)
                            else:
                                em.op("scalar", lambda e, g_=g, b=pb: e.activation(
                                    b[:, 512 * g_:512 * (g_ + 1)], ps_sc[g_][:, 0:512],
                                    AF.Relu, bias=cst[:, 3:4], scale=RSQ), inc=exd)
                            exd_k.append(exd.n)
                            if kt == 0:
                                em.wait("vector", exd, exd.n)
                                em.op("vector", lambda e, g_=g, b=pb: e.tensor_tensor(
                                    b[:, 512 * g_:512 * (g_ + 1)].rearrange("p (h q) -> p h q", q=128),
                                    b[:, 512 * g_:512 * (g_ + 1)].rearrange("p (h q) -> p h q", q=128),
                                    msk[:, :].rearrange("p (o q) -> p o q", o=1).broadcast_to([128, 4, 128]),
                                    ALU.mult), inc=mkd)
                                mkd_k.append(mkd.n)
                        if kt == 0:
                            em.wait("tensor", oev, oev_free)
                            em.wait("tensor", lrc, lrc_free)
                        lmm_k = []
                        for g in range(4):
                            if kt == 0:
                                em.wait("tensor", mkd, mkd_k[g])
                            else:
                                em.wait("tensor", exd, exd_k[g])
                            if layer == 0:
                                em.op("tensor", lambda e, g_=g, p=par, k=kt, l_=Lk, b=pb: e.matmul(
                                    ps_o[g_][:, 0:512],
                                    vsb[p][:, 130 * g_:130 * g_ + 128],
                                    b[:, 512 * g_:512 * (g_ + 1)],
                                    start=(k == 0), stop=(k == l_ - 1)))
                            else:
                                for h4 in range(4):
                                    h = 4 * g + h4
                                    em.op("tensor", lambda e, h_=h, g_=g, f=h4, p=par, k=kt, l_=Lk, b=pb: e.matmul(
                                        ps_o[g_][:, 128 * f:128 * (f + 1)],
                                        vsb[p][:, 130 * h_:130 * h_ + 128],
                                        b[:, 128 * h_:128 * (h_ + 1)],
                                        start=(k == 0 and f == 0), stop=(k == l_ - 1 and f == 3)))
                            em.op("tensor", lambda e, g_=g, p=par, b=pb: e.matmul(
                                ps_sc[g_][0:1, 0:512],
                                vsb[p][:, 128:129],
                                b[:, 512 * g_:512 * (g_ + 1)],
                                start=True, stop=True), inc=lmm)
                            lmm_k.append(lmm.n)
                        kv_free[par] = lmm.n
                        av_hist = [av_hist[1], lmm.n]
                        if kt == 0:
                            if layer == 0:
                                em.wait("vector", ltr, ltr.n)
                            else:
                                em.wait("vector", bcd, bcd.n)
                        for g in range(4):
                            em.wait("vector", lmm, lmm_k[g])
                            if kt == 0:
                                em.op("vector", lambda e, g_=g: e.tensor_copy(
                                    lsb[0:1, 512 * g_:512 * (g_ + 1)], ps_sc[g_][0:1, 0:512]),
                                    inc=lev)
                            else:
                                em.op("vector", lambda e, g_=g: e.tensor_tensor(
                                    lsb[0:1, 512 * g_:512 * (g_ + 1)], lsb[0:1, 512 * g_:512 * (g_ + 1)],
                                    ps_sc[g_][0:1, 0:512], ALU.add),
                                    inc=lev)
                    em.wait("scalar", lmm)
                    em.wait("scalar", stx if layer == 0 else cmb, ots_free)
                    for g in range(4):
                        em.op("scalar", lambda e, g_=g: e.copy(
                            oTs[:, 512 * g_:512 * (g_ + 1)], ps_o[g_][:, 0:512]), inc=oev)
                    if layer == 0:
                        em.wait("tensor", lev)
                        em.wait("tensor", exd)
                        em.wait("tensor", oev)
                        for h in range(16):
                            em.op("tensor", lambda e, h_=h: e.transpose(
                                ps_o[0][:, h_:h_ + 1], lsb[0:1, 128 * h_:128 * (h_ + 1)],
                                id_f[0:1, 0:1]), inc=(ltr if h == 15 else None))
                        for h in range(16):
                            em.op("tensor", lambda e, h_=h: e.transpose(
                                ps_sc[h_ // 4][:, 128 * (h_ % 4):128 * (h_ % 4) + 128],
                                oTs[:, 128 * h_:128 * (h_ + 1)], id_f[:, :]),
                                inc=(stx if h == 15 else None))
                        em.wait("vector", ltr)
                        em.op("vector", lambda e: e.reciprocal(lrq[:, :], ps_o[0][:, 0:16]), inc=lrc)
                        em.wait("vector", stx)
                        for h in range(16):
                            em.op("vector", lambda e, h_=h: e.tensor_scalar(
                                ssb[:, 128 * h_:128 * (h_ + 1)],
                                ps_sc[h_ // 4][:, 128 * (h_ % 4):128 * (h_ % 4) + 128],
                                lrq[:, h_:h_ + 1], None, ALU.mult),
                                inc=(cmb if h == 15 else None))
                        em.wait("sync", v2, v2.n)
                        em.wait("sync", st_g, marks["g_st0"])
                        em.dma("sync", xg[0][:, :], xsrc[128 * j:128 * (j + 1), :], ldg)
                        em.dma("sync", xg[1][:, :], gsrc[128 * j:128 * (j + 1), :], ldg)
                        em.wait("vector", ldg)
                        em.wait("vector", out_sems[0], 16 * j)
                        em.op("vector", lambda e: e.tensor_tensor(ost[:, :], ssb[:, :], xg[1][:, :], ALU.mult))
                        em.op("vector", lambda e: e.tensor_tensor(ost[:, :], ost[:, :], xg[0][:, :], ALU.add), inc=v2)
                        em.wait("sync", v2)
                        em.dma("sync", out_d[128 * j:128 * (j + 1), :], ost[:, :], out_sems[0])
                        ots_free = stx.n
                        lrc_free = lrc.n
                    else:
                        em.wait("vector", lev)
                        em.op("vector", lambda e: e.tensor_single_scalar(lsb[0:1, :], lsb[0:1, :], 1e-6, ALU.add))
                        em.op("vector", lambda e: e.reciprocal(lsb[0:1, :], lsb[0:1, :]), inc=lrc)
                        em.wait("tensor", lrc)
                        em.wait("tensor", exd)
                        for g in range(4):
                            em.op("tensor", lambda e, g_=g: e.matmul(
                                ps_sc[g_][:, 0:512], ones_f[0:1, 0:128],
                                lsb[0:1, 512 * g_:512 * (g_ + 1)],
                                start=True, stop=True), inc=bcd)
                        em.wait("vector", bcd)
                        em.wait("vector", oev)
                        for g in range(4):
                            em.op("vector", lambda e, g_=g, j_=j: e.tensor_tensor(
                                sTn[:, 2048 * j_ + 512 * g_:2048 * j_ + 512 * (g_ + 1)],
                                oTs[:, 512 * g_:512 * (g_ + 1)], ps_sc[g_][:, 0:512], ALU.mult),
                                inc=(cmb if g == 3 else None))
                        ots_free = cmb.n
                    sc_free = cmb.n
                    oev_free = oev.n

            if layer == 1:
                ps_mm = [esB.enter_context(nc.psum_tensor(L + f"mm{i}", [128, 512], F32)) for i in range(2)]
                em.wait("tensor", cmb)
                grp = [0]
                ev_free = [0, 0]
                for i in range(8):
                    if i == 0:
                        em.wait("tensor", wol)
                    em.wait("sync", v2, v2.n)
                    em.wait("sync", st_g, marks["g_st1"])
                    em.dma("sync", xg[0][:, :], xsrc[128 * i:128 * (i + 1), :], ldg)
                    em.dma("sync", xg[1][:, :], gsrc[128 * i:128 * (i + 1), :], ldg)
                    ldxg = ldg.n
                    em.wait("scalar", v2, v2.n)
                    for nb in range(4):
                        p2 = grp[0] % 2
                        grp[0] += 1
                        em.wait("tensor", evq, ev_free[p2])
                        for ccn in range(16):
                            em.op("tensor", lambda e, c=ccn, n=nb, p=p2, i_=i: e.matmul(
                                ps_mm[p][:, 0:512],
                                sTn[:, 2048 * i_ + 128 * c:2048 * i_ + 128 * (c + 1)],
                                wo_sb[:, 2048 * c + 512 * n:2048 * c + 512 * (n + 1)],
                                start=(c == 0), stop=(c == 15)), inc=(pe if ccn == 15 else None))
                        em.wait("scalar", pe)
                        em.op("scalar", lambda e, p=p2, c=512 * nb: e.copy(
                            ssb[:, c:c + 512], ps_mm[p][:, 0:512]), inc=evq)
                        ev_free[p2] = evq.n
                    em.wait("vector", evq)
                    em.wait("vector", ldg, ldxg)
                    em.wait("vector", out_sems[0], 16 * i)
                    em.op("vector", lambda e: e.tensor_tensor(ost[:, :], ssb[:, :], xg[1][:, :], ALU.mult))
                    em.op("vector", lambda e: e.tensor_tensor(ost[:, :], ost[:, :], xg[0][:, :], ALU.add), inc=v2)
                    em.wait("sync", v2)
                    em.dma("sync", out_d[128 * i:128 * (i + 1), :], ost[:, :], out_sems[0])

            if layer == 0:
                marks["x1_st0"] = out_sems[0].n
                marks["x1_st1"] = out_sems[1].n
            else:
                marks["xo_sems"] = (out_sems[0], out_sems[0].n, out_sems[1], out_sems[1].n)
            esB.close()

        def stage_barrier():
            em.barrier([v, scs, sc, pe, evq, evt, st, st_w, st_s[0], st_s[1], st_g,
                        st_o[0], st_o[1], ldk[0], ldk[1], gld])

        stage_prep()
        stage_barrier()
        stage_A(0)
        stage_barrier()
        stage_B(0)
        stage_barrier()
        stage_A(1)
        stage_barrier()
        stage_B(1)

        so0, n0, so1, n1 = marks["xo_sems"]
        em.wait("gpsimd", so0, n0)
        em.wait("gpsimd", so1, n1)
        em.finalize()
    return nc



_PROGS = {}


def _get_prog(key, builder, *args):
    if key not in _PROGS:
        _PROGS[key] = builder(*args)
    return _PROGS[key]


def _build_indices(c, layer):
    kidx = np.zeros((128, NKT), np.int32)
    vidx = np.zeros((128, NKT), np.int32)
    offs = slot_offsets()
    rmap = {}
    for o in range(8):
        for jo, (so, wo) in enumerate(own_tiles(o)):
            rmap[(so, wo)] = (o, jo)
    p = np.arange(128, dtype=np.int32)
    for j, (s, w) in enumerate(own_tiles(c)):
        koff, L = offs[j]
        order = [w] + list(range(w))
        for t in range(L):
            a = koff + t
            if t < len(order):
                o, jown = rmap[(s, order[t])]
                kidx[:, a] = o * 2176 + jown * 128 + p
                vidx[:, a] = o * 2176 + 1024 + jown * 128 + p
            else:
                kidx[:, a] = p
                vidx[:, a] = 2048 + p
    return kidx, vidx


def kernel(L0_attn_x, L0_attn_Wq, L0_attn_bq, L0_attn_Wk, L0_attn_bk, L0_attn_Wv,
           L0_attn_bv, L0_ffn_weight, L0_ffn_bias, L0_ffn_x_tensor_scale,
           L1_attn_Wq, L1_attn_Wk, L1_attn_Wv, L1_attn_Wo, L1_ffn_kernel, L1_ffn_bias):
    x = np.ascontiguousarray(np.asarray(L0_attn_x, np.float32))
    ident = np.eye(128, dtype=BF)
    ident_f = np.eye(128, dtype=np.float32)
    maskT = np.triu(np.ones((128, 128), dtype=np.float32)).astype(BF)
    cores = list(range(8))

    w0 = np.ascontiguousarray(np.asarray(L0_ffn_weight, np.float32))
    k1 = np.ascontiguousarray(np.asarray(L1_ffn_kernel, np.float32))
    wq1 = np.ascontiguousarray(np.asarray(L1_attn_Wq, np.float32))
    wk1 = np.ascontiguousarray(np.asarray(L1_attn_Wk, np.float32))
    wv1 = np.ascontiguousarray(np.asarray(L1_attn_Wv, np.float32))
    wo1 = np.ascontiguousarray(np.asarray(L1_attn_Wo, np.float32))

    nc = _get_prog("fused", build_fused)
    im = []
    for c in cores:
        sl = slice(256 * c, 256 * (c + 1))
        xo = np.ascontiguousarray(np.concatenate(
            [x[s, 128 * w:128 * (w + 1)] for (s, w) in own_tiles(c)], axis=0))
        k0i, v0i = _build_indices(c, 0)
        k1i, v1i = _build_indices(c, 1)
        im.append({
            "x_own": xo,
            "wq": np.ascontiguousarray(np.asarray(L0_attn_Wq, np.float32).transpose(1, 0, 2)),
            "wk": np.ascontiguousarray(np.asarray(L0_attn_Wk, np.float32).transpose(1, 0, 2)),
            "wv": np.ascontiguousarray(np.asarray(L0_attn_Wv, np.float32).transpose(1, 0, 2)),
            "bq_row": np.asarray(L0_attn_bq, np.float32).reshape(1, E),
            "bk_row": np.asarray(L0_attn_bk, np.float32).reshape(1, 512),
            "bv_row": np.asarray(L0_attn_bv, np.float32).reshape(1, 512),
            "ffn_bias0": np.asarray(L0_ffn_bias, np.float32).reshape(1, E),
            "xts": np.asarray(L0_ffn_x_tensor_scale, np.float32).reshape(1, 1),
            "ffn_bias1": np.asarray(L1_ffn_bias, np.float32).reshape(1, E),
            "w0_full": w0, "w0_own": np.ascontiguousarray(w0[sl]),
            "k1_own": np.ascontiguousarray(k1[:, sl]),
            "wq1_own": np.ascontiguousarray(wq1[sl]),
            "wk1_own": np.ascontiguousarray(wk1[sl]),
            "wv1_own": np.ascontiguousarray(wv1[sl]),
            "wo1_own": np.ascontiguousarray(wo1[sl]),
            "ident": ident, "ident_f": ident_f, "maskT": maskT,
            "kidx0": k0i, "vidx0": v0i, "kidx1": k1i, "vidx1": v1i,
        })
    res = run_bass_kernel_spmd(nc, im, core_ids=cores).results

    out = np.empty((B, S, E), np.float32)
    for c in cores:
        for j, (s, w) in enumerate(own_tiles(c)):
            out[s, 128 * w:128 * (w + 1)] = res[c]["xo"][128 * j:128 * (j + 1)]
    return out
